# version 03 (final)
# baseline (speedup 1.0000x reference)
from contextlib import ExitStack
import numpy as np
import concourse.bass as bass
import concourse.mybir as mybir
from concourse.bass_utils import run_bass_kernel_spmd

F32 = mybir.dt.float32
F32R = mybir.dt.float32r
I32 = mybir.dt.int32
AF = mybir.ActivationFunctionType
ALU = mybir.AluOpType
AX = mybir.AxisListType

D = 2048
SEQ = 4096
MINF = -1.0e30
KCNT = [4096, 3072, 2048, 1024]
QOFF = {0: [3584, 2560, 1536, 512], 1: [3072, 2048, 1024, 0]}
G = 4
DEV_JOBS = None
DEV_NO_RACE = False
DEV_SKIP = set()


class Buf:
    __slots__ = ("w", "readers", "name")

    def __init__(self, name=""):
        self.w = None
        self.readers = {}
        self.name = name


class SemRef:
    __slots__ = ("sem", "name", "n", "step", "qkey")

    def __init__(self, nc, name, step, qkey):
        self.sem = nc.alloc_semaphore(name=name)
        self.name = name
        self.n = 0
        self.step = step
        self.qkey = qkey


class Eng:
    __slots__ = ("o", "qkey", "sem", "name", "n", "step", "stream", "waited",
                 "is_dma")

    def __init__(self, nc, obj, qkey, name, step):
        self.o = obj
        self.qkey = qkey
        self.sem = nc.alloc_semaphore(name=name)
        self.name = name
        self.n = 0
        self.step = step
        self.stream = None
        self.waited = None
        self.is_dma = step == 16


class Tk:

    def __init__(self, nc):
        self.nc = nc
        self.pe = Eng(nc, nc.tensor, "pe", "t_pe", 1)
        self.dve = Eng(nc, nc.vector, "dve", "t_dve", 1)
        self.act = Eng(nc, nc.scalar, "act", "t_act", 1)
        self.pool = Eng(nc, nc.gpsimd, "gq", "t_pool", 1)
        self.sd = Eng(nc, nc.sync, "sd", "t_sd", 16)
        self.gd = Eng(nc, nc.gpsimd, "gq", "t_gd", 16)
        streams, waited = {}, {}
        for e in (self.pe, self.dve, self.act, self.pool, self.sd, self.gd):
            e.stream = streams.setdefault(e.qkey, [])
            e.waited = waited.setdefault(e.qkey, {})
        self.streams = streams
        self.dma_refs = {}
        self.nsem = 0

    def _dma_ref(self, eng, buf):
        key = (eng.qkey, buf.name)
        r = self.dma_refs.get(key)
        if r is None:
            self.nsem += 1
            r = SemRef(self.nc, f"d{self.nsem}_{buf.name[:12]}", 16, eng.qkey)
            self.dma_refs[key] = r
        return r

    def emit(self, eng, fn, reads=(), writes=()):
        need = {}

        def add(ref, v):
            if ref is eng and eng.qkey == "pe":
                return
            need[ref] = max(need.get(ref, 0), v)

        for b in reads:
            if b.w is not None:
                add(*b.w)
        for b in writes:
            for ref, rv in b.readers.items():
                add(ref, rv)
            if b.w is not None:
                add(*b.w)
        for ref, wv in need.items():
            if wv > eng.waited.get(ref.name, 0):
                eng.waited[ref.name] = wv
                eng.stream.append(lambda o=eng.o, s=ref.sem, v=wv: o.wait_ge(s, v))
        if eng.is_dma:
            ref = self._dma_ref(eng, writes[0] if writes else
                                (reads[0] if reads else Buf("misc")))
        else:
            ref = eng
        ref.n += ref.step
        mv = ref.n
        eng.stream.append(lambda f=fn, s=ref.sem, v=ref.step: f().then_inc(s, v))
        for b in writes:
            b.w = (ref, mv)
            b.readers = {}
        for b in reads:
            b.readers[ref] = mv
        return mv

    def barrier(self):
        qrefs = {}
        for (qk, _), r in self.dma_refs.items():
            qrefs.setdefault(qk, []).append(r)
        engs = (self.pe, self.dve, self.act, self.pool, self.sd)
        toks = []
        for e in engs:
            for r in qrefs.get(e.qkey, []):
                if r.n > e.waited.get(r.name, 0):
                    e.waited[r.name] = r.n
                    e.stream.append(
                        lambda o=e.o, s=r.sem, v=r.n: o.wait_ge(s, v))
            b = Buf("tok_" + e.name)
            self.emit(e, lambda o=e.o: o.drain(), reads=(), writes=(b,))
            toks.append(b)
        for e in engs:
            self.emit(e, lambda o=e.o: o.nop(), reads=tuple(toks), writes=())
        self.flush()

    def flush(self):
        nc = self.nc
        s = self.streams
        with nc.Block() as block:
            if s.get("sd"):
                @block.sync
                def _(_e, th=list(s["sd"])):
                    for f in th:
                        f()
            if s.get("pe"):
                @block.tensor
                def _(_e, th=list(s["pe"])):
                    for f in th:
                        f()
            if s.get("dve"):
                @block.vector
                def _(_e, th=list(s["dve"])):
                    for f in th:
                        f()
            if s.get("act"):
                @block.scalar
                def _(_e, th=list(s["act"])):
                    for f in th:
                        f()
            if s.get("gq"):
                @block.gpsimd
                def _(_e, th=list(s["gq"])):
                    for f in th:
                        f()
        for v in s.values():
            v.clear()

def host_tile(w):
    Kin, Mout = w.shape
    a = w.reshape(Kin // 128, 128, Mout // 128, 128).transpose(2, 1, 0, 3)
    return np.ascontiguousarray(a).reshape(Mout // 128, 128, Kin)


def host_cols(b):
    return np.ascontiguousarray(b.reshape(-1, 128).T)


def make_masks(parity):
    out = np.zeros((4, 8, 128, 512), dtype=np.float32)
    for j in range(4):
        q0 = QOFF[parity][j]
        base = KCNT[j] - 8 * 128
        for w in range(8):
            kk = base + w * 128 + np.arange(128)[:, None]
            qq = q0 + np.arange(512)[None, :]
            out[j, w] = np.where(kk <= qq, 0.0, MINF)
    return out


CONSTS = {
    "ident": np.eye(128, dtype=np.float32),
    "ones_c128": np.ones((128, 1), dtype=np.float32),
    "ones_r128": np.ones((1, 128), dtype=np.float32),
    "ones_c64": np.ones((64, 1), dtype=np.float32),
    "e2": np.concatenate([np.repeat([[1.0, 0.0]], 64, 0),
                          np.repeat([[0.0, 1.0]], 64, 0)], 0).astype(np.float32),
}


class Ctx:

    def __init__(self):
        self.nc = bass.Bass("TRN2", target_bir_lowering=False, debug=False,
                            detect_race_conditions=not DEV_NO_RACE)
        self.tk = Tk(self.nc)
        self.t = {}
        self.b = {}

    def buf(self, name):
        if name not in self.b:
            self.b[name] = Buf(name)
        return self.b[name]

    def dram(self, name, shape, dtype=F32, kind="Internal"):
        self.t[name] = self.nc.dram_tensor(name, list(shape), dtype, kind=kind).ap()
        return self.t[name]

    def load_consts(self):
        nc, tk = self.nc, self.tk
        self.c_ident = nc.alloc_sbuf_tensor("c_ident", [128, 128], F32).ap()
        self.c_oc128 = nc.alloc_sbuf_tensor("c_oc128", [128, 1], F32R).ap()
        self.c_or128 = nc.alloc_sbuf_tensor("c_or128", [1, 128], F32R).ap()
        self.c_oc64 = nc.alloc_sbuf_tensor("c_oc64", [64, 1], F32R).ap()
        self.c_e2 = nc.alloc_sbuf_tensor("c_e2", [128, 2], F32R).ap()
        self.c_eps5 = nc.alloc_sbuf_tensor("c_eps5", [128, 1], F32).ap()
        self.c_m128 = nc.alloc_sbuf_tensor("c_m128", [128, 1], F32).ap()
        h = {k: self.nc.inline_tensor(np.ascontiguousarray(v), name="ct_" + k)
             for k, v in CONSTS.items()}
        cb = self.buf("consts")
        tk.emit(tk.sd, lambda: nc.sync.dma_start(out=self.c_ident, in_=h["ident"].ap()),
                writes=(cb,))
        tk.emit(tk.sd, lambda: nc.sync.dma_start(
            out=self.c_oc128, in_=h["ones_c128"].ap().bitcast(F32R)), writes=(cb,))
        tk.emit(tk.sd, lambda: nc.sync.dma_start(
            out=self.c_or128, in_=h["ones_r128"].ap().bitcast(F32R)), writes=(cb,))
        tk.emit(tk.sd, lambda: nc.sync.dma_start(
            out=self.c_oc64, in_=h["ones_c64"].ap().bitcast(F32R)), writes=(cb,))
        tk.emit(tk.sd, lambda: nc.sync.dma_start(
            out=self.c_e2, in_=h["e2"].ap().bitcast(F32R)), writes=(cb,))
        tk.emit(tk.dve, lambda: nc.vector.memset(self.c_eps5, 1e-5), writes=(cb,))
        tk.emit(tk.dve, lambda: nc.vector.memset(self.c_m128, -128.0), writes=(cb,))


def emit_ln(ctx, xa, bxa, xsq, bxsq, stats, bst, sub):
    nc, tk = ctx.nc, ctx.tk
    a = xa[:, sub * 2048:(sub + 1) * 2048]
    c0 = (sub % 4) * 8
    s1 = stats[:, c0:c0 + 1]
    s2 = stats[:, c0 + 1:c0 + 2]
    negm = stats[:, c0 + 2:c0 + 3]
    var = stats[:, c0 + 3:c0 + 4]
    sd_ = stats[:, c0 + 4:c0 + 5]
    r = stats[:, c0 + 5:c0 + 6]
    msq = stats[:, c0 + 6:c0 + 7]
    tk.emit(tk.act, lambda: nc.scalar.activation(out=xsq, in_=a, func=AF.Square),
            reads=(bxa,), writes=(bxsq,))
    tk.emit(tk.dve, lambda: nc.vector.reduce_sum(s1, a, axis=AX.X),
            reads=(bxa,), writes=(bst,))
    tk.emit(tk.dve, lambda: nc.vector.reduce_sum(s2, xsq, axis=AX.X),
            reads=(bxsq,), writes=(bst,))
    tk.emit(tk.dve, lambda: nc.vector.tensor_scalar_mul(negm, s1, -1.0 / 2048.0),
            reads=(bst,), writes=(bst,))
    tk.emit(tk.dve, lambda: nc.vector.tensor_mul(msq, negm, negm),
            reads=(bst,), writes=(bst,))
    tk.emit(tk.dve, lambda: nc.vector.scalar_tensor_tensor(
        out=var, in0=s2, scalar=1.0 / 2048.0, in1=msq,
        op0=ALU.mult, op1=ALU.subtract), reads=(bst,), writes=(bst,))
    tk.emit(tk.act, lambda: nc.scalar.activation(
        out=sd_, in_=var, func=AF.Sqrt, bias=ctx.c_eps5[:, 0:1]),
        reads=(bst, ctx.buf("consts")), writes=(bst,))
    tk.emit(tk.dve, lambda: nc.vector.reciprocal(r, sd_), reads=(bst,), writes=(bst,))
    tk.emit(tk.dve, lambda: nc.vector.tensor_scalar(
        out=a, in0=a, scalar1=negm, scalar2=r, op0=ALU.add, op1=ALU.mult),
        reads=(bst, bxa), writes=(bxa,))


def emit_transpose_split(ctx, xa, bxa, psT, bpsT, dsts, bdsts, split,
                         subs=None):
    nc, tk = ctx.nc, ctx.tk
    if subs is None:
        subs = [(s, s) for s in range(4)]
    for (xs, gs) in subs:
        for c in range(16):
            alt = (gs * 16 + c) % 2
            src = xa[:, xs * 2048 + c * 128: xs * 2048 + c * 128 + 128]
            tk.emit(tk.pe, lambda s=src, a=alt: nc.tensor.transpose(
                psT[a][:, 0:128], s, ctx.c_ident),
                reads=(bxa, ctx.buf("consts")), writes=(bpsT[alt],))
            hi = dsts[0][:, c * 512 + gs * 128: c * 512 + gs * 128 + 128]
            tk.emit(tk.dve, lambda h=hi, a=alt: nc.vector.tensor_copy(
                h, psT[a][:, 0:128]), reads=(bpsT[alt],), writes=(bdsts[0],))
            if split:
                lo = dsts[1][:, c * 512 + gs * 128: c * 512 + gs * 128 + 128]
                tk.emit(tk.dve, lambda l=lo, h=hi, a=alt: nc.vector.tensor_sub(
                    l, psT[a][:, 0:128], h),
                    reads=(bpsT[alt], bdsts[0]), writes=(bdsts[1],))


def emit_wsplit(ctx, eng, wbuf_ap, bw, whi_ap, bhi, wlo_ap, blo):
    nc, tk = ctx.nc, ctx.tk
    e = eng.o
    tk.emit(eng, lambda: e.tensor_copy(whi_ap, wbuf_ap), reads=(bw,), writes=(bhi,))
    tk.emit(eng, lambda: e.tensor_sub(wlo_ap, wbuf_ap, whi_ap),
            reads=(bw, bhi), writes=(blo,))


def emit_phase0a(ctx):
    nc, tk, t = ctx.nc, ctx.tk, ctx.t
    ctx.es_attn = ExitStack()
    mk2 = ctx.es_attn.enter_context(
        nc.sbuf_tensor("a_mk2", [1, 8], F32)).ap()
    es = ExitStack()
    def al(name, shape, dtype):
        return es.enter_context(nc.sbuf_tensor(name, shape, dtype)).ap()
    def alp(name):
        return es.enter_context(nc.psum_tensor(name, [128, 512], F32)).ap()
    xa = al("a_xa", [128, 4 * 2048], F32)
    xsq = al("a_xsq", [128, 2048], F32)
    stats = al("a_stats", [128, 32], F32)
    xnfh = al("a_xnfh", [128, 16 * 512], F32R)
    xnfl = al("a_xnfl", [128, 16 * 512], F32R)
    wbuf = al("a_wbuf", [128, 2 * 2048], F32)
    whi = al("a_whi", [128, 2 * 2048], F32R)
    wlo = al("a_wlo", [128, 2 * 2048], F32R)
    wvhi = al("a_wvhi", [128, 8 * 512], F32R)
    wvlo = al("a_wvlo", [128, 8 * 512], F32R)
    kst = al("a_kst", [128, 1024], F32R)
    vst = al("a_vst", [128, 2 * 1024], F32R)
    qst = al("a_qst", [128, 512], F32)
    ksq = al("a_ksq", [128, 512], F32R)
    nk2t = al("a_nk2t", [2, 1], F32)
    mk2st = al("a_mk2st", [2, 4], F32)
    psT = [alp("a_psT0"), alp("a_psT1")]
    psP = [alp("a_psP0"), alp("a_psP1")]
    psN = alp("a_psN")

    B = ctx.buf
    bxa, bxsq, bst = B("a_xa"), B("a_xsq"), B("a_st")
    bxnfh, bxnfl = B("a_xnfh"), B("a_xnfl")
    bw = [B("a_w0"), B("a_w1")]
    bhi = [B("a_hi0"), B("a_hi1")]
    blo = [B("a_lo0"), B("a_lo1")]
    bwv = B("a_wv")
    bkst = [B("a_kst0"), B("a_kst1")]
    bvst = [B("a_vst0"), B("a_vst1")]
    bqst = [B("a_qst0"), B("a_qst1")]
    bksq = B("a_ksq")
    bnk2, bmk2st = B("a_nk2"), B("a_mk2st")
    bpsT = [B("a_psT0"), B("a_psT1")]
    bpsP = [B("a_psP0"), B("a_psP1")]
    bpsN = B("a_psN")
    bKH, bKL = B("h_kh"), B("h_kl")
    bVH, bVL = B("h_vh"), B("h_vl")
    bQ = B("h_q")
    bMK2 = B("a_mk2")

    jobs = [(False, ci) for ci in range(8)] + [(True, j) for j in range(4)]
    if DEV_JOBS is not None:
        jobs = jobs[:DEV_JOBS]

    tk.emit(tk.dve, lambda: nc.vector.memset(mk2st, 0.0), writes=(bmk2st,))
    for ji, (isq, ci) in enumerate(jobs):
        src = t["xown"] if isq else t["xseq"]
        tk.emit(tk.sd, lambda s=src, c=ci: nc.sync.dma_start(
            out=xa.rearrange("p (s d) -> p s d", s=4),
            in_=s[c * 512:(c + 1) * 512, :].rearrange(
                "(s p) d -> p s d", p=128)), writes=(bxa,))
        if "ln" not in DEV_SKIP:
            for sub in range(4):
                emit_ln(ctx, xa, bxa, xsq, bxsq, stats, bst, sub)
        if "tsp" not in DEV_SKIP:
            emit_transpose_split(ctx, xa, bxa, psT, bpsT,
                                 (xnfh, xnfl), (bxnfh, bxnfl), split=True)
        if not isq and "K" not in DEV_SKIP:
            for dc in range(4):
                sl = dc % 2
                wslice = wbuf[:, sl * 2048: sl * 2048 + 2048]
                tk.emit(tk.sd, lambda d=dc, w=wslice: nc.sync.dma_start(
                    out=w, in_=t["wk_t"][d]), writes=(bw[sl],))
                emit_wsplit(ctx, tk.pool, wslice, bw[sl],
                            whi[:, sl * 2048: sl * 2048 + 2048], bhi[sl],
                            wlo[:, sl * 2048: sl * 2048 + 2048], blo[sl])
                pp = dc % 2
                for c in range(16):
                    lh = whi[:, sl * 2048 + c * 128: sl * 2048 + c * 128 + 128]
                    ll = wlo[:, sl * 2048 + c * 128: sl * 2048 + c * 128 + 128]
                    rh = xnfh[:, c * 512:(c + 1) * 512]
                    rl = xnfl[:, c * 512:(c + 1) * 512]
                    st = (c == 0)
                    tk.emit(tk.pe, lambda a=lh, b_=rh, s=st: nc.tensor.matmul(
                        psP[pp], a, b_, start=s, stop=False, skip_group_check=True),
                        reads=(bhi[sl], bxnfh), writes=(bpsP[pp],))
                    tk.emit(tk.pe, lambda a=lh, b_=rl: nc.tensor.matmul(
                        psP[pp], a, b_, start=False, stop=False, skip_group_check=True),
                        reads=(bhi[sl], bxnfl), writes=(bpsP[pp],))
                    tk.emit(tk.pe, lambda a=ll, b_=rh, s=(c == 15): nc.tensor.matmul(
                        psP[pp], a, b_, start=False, stop=s, skip_group_check=True),
                        reads=(blo[sl], bxnfh), writes=(bpsP[pp],))
                if "Kevac" not in DEV_SKIP:
                    tk.emit(tk.act, lambda: nc.scalar.activation(
                        out=kst[:, 0:512], in_=psP[pp], func=AF.Identity),
                        reads=(bpsP[pp],), writes=(bkst[0],))
                if "Kevac" not in DEV_SKIP:
                    tk.emit(tk.dve, lambda: nc.vector.tensor_sub(
                        kst[:, 512:1024], psP[pp], kst[:, 0:512]),
                        reads=(bpsP[pp], bkst[0]), writes=(bkst[0],))
                if "Ksq" not in DEV_SKIP:
                    tk.emit(tk.act, lambda: nc.scalar.activation(
                        out=ksq, in_=kst[:, 0:512], func=AF.Square),
                        reads=(bkst[0],), writes=(bksq,))
                if "nrm" in DEV_SKIP:
                    pass
                else:
                    tk.emit(tk.pe, lambda: nc.tensor.matmul(
                        psN[0:2, :], ctx.c_e2, ksq, start=True, stop=True, skip_group_check=True),
                        reads=(bksq, B("consts")), writes=(bpsN,))
                if "nrm" not in DEV_SKIP:
                    tk.emit(tk.dve, lambda: nc.vector.reduce_max(
                        nk2t, psN[0:2, :], axis=AX.X), reads=(bpsN,), writes=(bnk2,))
                    tk.emit(tk.dve, lambda d=dc: nc.vector.tensor_max(
                        mk2st[:, d:d + 1], mk2st[:, d:d + 1], nk2t),
                        reads=(bnk2, bmk2st), writes=(bmk2st,))
                for hh in (() if ("Kst" in DEV_SKIP or "Kevac" in DEV_SKIP) else range(2)):
                    kv = 2 * dc + hh
                    srch = kst[hh * 64: hh * 64 + 64, 0:512]
                    srcl = kst[hh * 64: hh * 64 + 64, 512:1024]
                    tk.emit(tk.gd, lambda s=srch, k=kv, c=ci: nc.gpsimd.dma_start(
                        out=t["kh_hbm"][:, k * 4096 + c * 512: k * 4096 + c * 512 + 512],
                        in_=s), reads=(bkst[0],), writes=(bKH,))
                    tk.emit(tk.gd, lambda s=srcl, k=kv, c=ci: nc.gpsimd.dma_start(
                        out=t["kl_hbm"][:, k * 4096 + c * 512: k * 4096 + c * 512 + 512],
                        in_=s), reads=(bkst[0],), writes=(bKL,))
            pass
        if (not isq) and "V" not in DEV_SKIP:
            psV = [psP[0], psP[1], psT[0], psT[1]]
            bpsV = [bpsP[0], bpsP[1], bpsT[0], bpsT[1]]
            for half in range(2):
                for c8 in range(8):
                    c = half * 8 + c8
                    sl = c % 2
                    wslice = wbuf[:, sl * 2048: sl * 2048 + 512]
                    tk.emit(tk.sd, lambda cc=c, w=wslice: nc.sync.dma_start(
                        out=w, in_=t["wv_nat"][cc]), writes=(bw[sl],))
                    tk.emit(tk.pool, lambda cc=c8, w=wslice: nc.gpsimd.tensor_copy(
                        wvhi[:, cc * 512:(cc + 1) * 512], w),
                        reads=(bw[sl],), writes=(bwv,))
                    tk.emit(tk.pool, lambda cc=c8, w=wslice: nc.gpsimd.tensor_sub(
                        wvlo[:, cc * 512:(cc + 1) * 512], w,
                        wvhi[:, cc * 512:(cc + 1) * 512]),
                        reads=(bw[sl],), writes=(bwv,))
                for ts in range(4):
                    for c8 in range(8):
                        c = half * 8 + c8
                        lh = xnfh[:, c * 512 + ts * 128: c * 512 + ts * 128 + 128]
                        ll = xnfl[:, c * 512 + ts * 128: c * 512 + ts * 128 + 128]
                        rh = wvhi[:, c8 * 512:(c8 + 1) * 512]
                        rl = wvlo[:, c8 * 512:(c8 + 1) * 512]
                        st0 = (half == 0 and c8 == 0)
                        st1 = (half == 1 and c8 == 7)
                        tk.emit(tk.pe, lambda a=lh, b_=rh, s=st0, v=ts: nc.tensor.matmul(
                            psV[v], a, b_, start=s, stop=False, skip_group_check=True),
                            reads=(bxnfh, bwv), writes=(bpsV[ts],))
                        tk.emit(tk.pe, lambda a=lh, b_=rl, v=ts: nc.tensor.matmul(
                            psV[v], a, b_, start=False, stop=False, skip_group_check=True),
                            reads=(bxnfh, bwv), writes=(bpsV[ts],))
                        tk.emit(tk.pe, lambda a=ll, b_=rh, s=st1, v=ts: nc.tensor.matmul(
                            psV[v], a, b_, start=False, stop=s, skip_group_check=True),
                            reads=(bxnfl, bwv), writes=(bpsV[ts],))
            for ts in range(4):
                sl = ts % 2
                dsth = vst[:, sl * 1024: sl * 1024 + 512]
                dstl = vst[:, sl * 1024 + 512: sl * 1024 + 1024]
                tk.emit(tk.act, lambda d=dsth, v=ts: nc.scalar.activation(
                    out=d, in_=psV[v], func=AF.Identity),
                    reads=(bpsV[ts],), writes=(bvst[sl],))
                tk.emit(tk.dve, lambda dl=dstl, dh=dsth, v=ts: nc.vector.tensor_sub(
                    dl, psV[v], dh), reads=(bpsV[ts], bvst[sl]), writes=(bvst[sl],))
                r0 = ci * 512 + ts * 128
                tk.emit(tk.gd, lambda s=dsth, r=r0: nc.gpsimd.dma_start(
                    out=t["vh_hbm"][r:r + 128, :], in_=s),
                    reads=(bvst[sl],), writes=(bVH,))
                tk.emit(tk.gd, lambda s=dstl, r=r0: nc.gpsimd.dma_start(
                    out=t["vl_hbm"][r:r + 128, :], in_=s),
                    reads=(bvst[sl],), writes=(bVL,))
        else:
            for dc in range(16):
                sl = dc % 2
                wslice = wbuf[:, sl * 2048: sl * 2048 + 2048]
                tk.emit(tk.sd, lambda d=dc, w=wslice: nc.sync.dma_start(
                    out=w, in_=t["wq_t"][d]), writes=(bw[sl],))
                emit_wsplit(ctx, tk.pool, wslice, bw[sl],
                            whi[:, sl * 2048: sl * 2048 + 2048], bhi[sl],
                            wlo[:, sl * 2048: sl * 2048 + 2048], blo[sl])
                pp = dc % 2
                for c in range(16):
                    lh = whi[:, sl * 2048 + c * 128: sl * 2048 + c * 128 + 128]
                    ll = wlo[:, sl * 2048 + c * 128: sl * 2048 + c * 128 + 128]
                    rh = xnfh[:, c * 512:(c + 1) * 512]
                    rl = xnfl[:, c * 512:(c + 1) * 512]
                    tk.emit(tk.pe, lambda a=lh, b_=rh, s=(c == 0): nc.tensor.matmul(
                        psP[pp], a, b_, start=s, stop=False, skip_group_check=True),
                        reads=(bhi[sl], bxnfh), writes=(bpsP[pp],))
                    tk.emit(tk.pe, lambda a=lh, b_=rl: nc.tensor.matmul(
                        psP[pp], a, b_, start=False, stop=False, skip_group_check=True),
                        reads=(bhi[sl], bxnfl), writes=(bpsP[pp],))
                    tk.emit(tk.pe, lambda a=ll, b_=rh, s=(c == 15): nc.tensor.matmul(
                        psP[pp], a, b_, start=False, stop=s, skip_group_check=True),
                        reads=(blo[sl], bxnfh), writes=(bpsP[pp],))
                tk.emit(tk.act, lambda: nc.scalar.activation(
                    out=qst, in_=psP[pp], func=AF.Identity, scale=0.125),
                    reads=(bpsP[pp],), writes=(bqst[0],))
                tk.emit(tk.gd, lambda d=dc, j=ci: nc.gpsimd.dma_start(
                    out=t["q_hbm"][:, d * 2048 + j * 512: d * 2048 + j * 512 + 512],
                    in_=qst), reads=(bqst[0],), writes=(bQ,))
        if ji == min(7, len(jobs) - 1) and not isq:
            for r_ in range(2):
                tk.emit(tk.gd, lambda r=r_: nc.gpsimd.dma_start(
                    out=mk2[0:1, r * 4:(r + 1) * 4], in_=mk2st[r:r + 1, :]),
                    reads=(bmk2st,), writes=(bMK2,))
    ctx.mk2 = mk2
    es.close()
    tk.barrier()


def emit_phase0b(ctx):
    nc, tk, t = ctx.nc, ctx.tk, ctx.t
    B = ctx.buf
    mk2 = ctx.mk2

    ysb = ctx.es_attn.enter_context(
        nc.sbuf_tensor("b_ysb", [128, 16 * 512], F32)).ap()
    yhi = ctx.es_attn.enter_context(
        nc.sbuf_tensor("b_yhi", [128, 16 * 512], F32R)).ap()

    by = B("b_ysb")
    byhi = B("b_yhi")
    bX1 = B("h_x1")

    for j in range(4):
        nkc = KCNT[j] // 128
        nkb = KCNT[j] // 512
        wbase = nkc - 8

        esA = ExitStack()

        def al(name, shape, dtype, _es=esA, _j=j):
            return _es.enter_context(nc.sbuf_tensor(f"{name}_{_j}", shape, dtype)).ap()

        def alp(name, _es=esA, _j=j):
            return _es.enter_context(
                nc.psum_tensor(f"{name}_{_j}", [128, 512], F32)).ap()

        qf = al("b_qf", [64, 2 * 512], F32)
        qhi = al("b_qhi", [64, G * 512], F32R)
        qlo = al("b_qlo", [64, G * 512], F32R)
        qsq = al("b_qsq", [64, 512], F32R)
        negc = al("b_negc", [1, G * 512], F32R)
        khb = al("b_khb", [64, 8 * 512], F32R)
        klb = al("b_klb", [64, 8 * 512], F32R)
        vhb = al("b_vhb", [128, 4 * 512], F32R)
        vlb = al("b_vlb", [128, 4 * 512], F32R)
        mskb = al("b_msk", [128, 4 * 512], F32)
        p32 = al("b_p32", [128, 2 * 512], F32)
        phi = al("b_phi", [128, 2 * 512], F32R)
        plo = al("b_plo", [128, 2 * 512], F32R)
        spart = al("b_spart", [128, G * 512], F32)
        sphi = al("b_sphi", [128, 512], F32R)
        splo = al("b_splo", [128, 512], F32R)
        r32 = al("b_r32", [1, 512], F32)
        rhi = al("b_rhi", [1, 512], F32R)
        rlo = al("b_rlo", [1, 512], F32R)
        bvw = al("b_bvw", [128, 512], F32)
        psS = [alp("b_psS0"), alp("b_psS1")]
        psAV = alp("b_psAV")
        psD = alp("b_psD")
        psB = alp("b_psB")

        bqf = [B("b_qf0"), B("b_qf1")]
        bqrhs = B("b_qrhs")
        bqsq = B("b_qsq")
        bkb, bvb = B("b_kb"), B("b_vb")
        bmsk = B("b_msk")
        bp32 = [B("b_p320"), B("b_p321")]
        bphi = [B("b_phi0"), B("b_phi1")]
        bplo = [B("b_plo0"), B("b_plo1")]
        bsp = B("b_spart")
        bspl = B("b_sphilo")
        brr = B("b_r")
        bbv = B("b_bvw")
        bpsS = [B("b_psS0"), B("b_psS1")]
        bpsAV = B("b_psAV")
        bpsD = B("b_psD")
        bpsB = B("b_psB")

        if j == 0:
            tk.emit(tk.dve, lambda: nc.vector.memset(ysb, 0.0), writes=(by,))
        else:
            tk.emit(tk.dve, lambda: nc.vector.memset(ysb, 0.0), reads=(byhi,),
                    writes=(by,))
        for hg in range(32 // G):
            tk.emit(tk.pool, lambda: nc.gpsimd.memset(spart, 0.0),
                    reads=(), writes=(bsp,))
            for h8 in range(G):
                h = hg * G + h8
                sl = h8 % 2
                qsl = qf[:, sl * 512: sl * 512 + 512]
                tk.emit(tk.gd, lambda hh=h, jj=j, q=qsl: nc.gpsimd.dma_start(
                    out=q, in_=t["q_hbm"][(hh % 2) * 64:(hh % 2) * 64 + 64,
                                          (hh // 2) * 2048 + jj * 512:
                                          (hh // 2) * 2048 + jj * 512 + 512]),
                    reads=(B("h_q"),), writes=(bqf[sl],))
                dqh = qhi[:, h8 * 512:(h8 + 1) * 512]
                dql = qlo[:, h8 * 512:(h8 + 1) * 512]
                tk.emit(tk.dve, lambda a=dqh, q=qsl: nc.vector.tensor_copy(a, q),
                        reads=(bqf[sl],), writes=(bqrhs,))
                tk.emit(tk.dve, lambda a=dql, q=qsl, hh=dqh: nc.vector.tensor_sub(
                    a, q, hh), reads=(bqf[sl],), writes=(bqrhs,))
                tk.emit(tk.act, lambda q=qsl: nc.scalar.activation(
                    out=qsq, in_=q, func=AF.Square), reads=(bqf[sl],), writes=(bqsq,))
                tk.emit(tk.pe, lambda: nc.tensor.matmul(
                    psD[0:1, :], ctx.c_oc64, qsq, start=True, stop=True,
                    skip_group_check=True),
                    reads=(bqsq, B("consts")), writes=(bpsD,))
                kv = h // 4
                kcol = (kv % 2) * 4 + kv // 2
                tk.emit(tk.act, lambda k=kcol: nc.scalar.activation(
                    out=r32, in_=psD[0:1, :], func=AF.Sqrt,
                    scale=mk2[0:1, k:k + 1]),
                    reads=(bpsD, B("a_mk2")), writes=(brr,))
                dnc = negc[:, h8 * 512:(h8 + 1) * 512]
                tk.emit(tk.dve, lambda d=dnc: nc.vector.tensor_scalar_mul(
                    d, r32, -1.0), reads=(brr,), writes=(bqrhs,))
            for kb in range(nkb):
                tk.emit(tk.sd, lambda b_=kb: nc.sync.dma_start(
                    out=khb.rearrange("p (h t) -> p h t", h=8),
                    in_=t["kh_hbm"].rearrange("p (h t) -> p h t", h=8)[
                        :, :, b_ * 512:(b_ + 1) * 512]), writes=(bkb,))
                tk.emit(tk.sd, lambda b_=kb: nc.sync.dma_start(
                    out=klb.rearrange("p (h t) -> p h t", h=8),
                    in_=t["kl_hbm"].rearrange("p (h t) -> p h t", h=8)[
                        :, :, b_ * 512:(b_ + 1) * 512]), writes=(bkb,))
                tk.emit(tk.sd, lambda b_=kb: nc.sync.dma_start(
                    out=vhb.rearrange("p (s d) -> p s d", s=4),
                    in_=t["vh_hbm"][b_ * 512:(b_ + 1) * 512, :].rearrange(
                        "(s p) d -> p s d", p=128)), writes=(bvb,))
                tk.emit(tk.sd, lambda b_=kb: nc.sync.dma_start(
                    out=vlb.rearrange("p (s d) -> p s d", s=4),
                    in_=t["vl_hbm"][b_ * 512:(b_ + 1) * 512, :].rearrange(
                        "(s p) d -> p s d", p=128)), writes=(bvb,))
                if kb >= nkb - 2:
                    wlo_ = (kb - (nkb - 2)) * 4
                    tk.emit(tk.sd, lambda jj=j, w0=wlo_: nc.sync.dma_start(
                        out=mskb.rearrange("p (w t) -> p w t", w=4),
                        in_=t["masks"][jj].rearrange("w p t -> p w t")[
                            :, w0:w0 + 4, :]), writes=(bmsk,))
                for h8 in range(G):
                    h = hg * G + h8
                    kv = h // 4
                    for kc in range(4):
                        g = kb * 4 + kc
                        sp = g % 2
                        lkh = khb[:, kv * 512 + kc * 128: kv * 512 + kc * 128 + 128]
                        lkl = klb[:, kv * 512 + kc * 128: kv * 512 + kc * 128 + 128]
                        rqh = qhi[:, h8 * 512:(h8 + 1) * 512]
                        rql = qlo[:, h8 * 512:(h8 + 1) * 512]
                        rnc = negc[:, h8 * 512:(h8 + 1) * 512]
                        tk.emit(tk.pe, lambda a=lkh, b_=rqh, s=sp: nc.tensor.matmul(
                            psS[s], a, b_, start=True, stop=False,
                            skip_group_check=True),
                            reads=(bkb, bqrhs), writes=(bpsS[sp],))
                        tk.emit(tk.pe, lambda a=lkh, b_=rql, s=sp: nc.tensor.matmul(
                            psS[s], a, b_, start=False, stop=False,
                            skip_group_check=True),
                            reads=(bkb, bqrhs), writes=(bpsS[sp],))
                        tk.emit(tk.pe, lambda a=lkl, b_=rqh, s=sp: nc.tensor.matmul(
                            psS[s], a, b_, start=False, stop=False,
                            skip_group_check=True),
                            reads=(bkb, bqrhs), writes=(bpsS[sp],))
                        tk.emit(tk.pe, lambda b_=rnc, s=sp: nc.tensor.matmul(
                            psS[s], ctx.c_or128, b_, start=False, stop=True,
                            skip_group_check=True),
                            reads=(bqrhs, B("consts")), writes=(bpsS[sp],))
                        if g >= wbase:
                            w = g % 4
                            tk.emit(tk.dve, lambda s=sp, ww=w: nc.vector.tensor_add(
                                psS[s], psS[s],
                                mskb[:, ww * 512:(ww + 1) * 512]),
                                reads=(bmsk, bpsS[sp]), writes=(bpsS[sp],))
                        tk.emit(tk.act, lambda s=sp: nc.scalar.activation(
                            out=p32[:, s * 512:(s + 1) * 512], in_=psS[s],
                            func=AF.Exp), reads=(bpsS[sp],), writes=(bp32[sp],))
                        tk.emit(tk.dve, lambda s=sp: nc.vector.tensor_copy(
                            phi[:, s * 512:(s + 1) * 512],
                            p32[:, s * 512:(s + 1) * 512]),
                            reads=(bp32[sp],), writes=(bphi[sp],))
                        tk.emit(tk.dve, lambda s=sp: nc.vector.tensor_sub(
                            plo[:, s * 512:(s + 1) * 512],
                            p32[:, s * 512:(s + 1) * 512],
                            phi[:, s * 512:(s + 1) * 512]),
                            reads=(bp32[sp], bphi[sp]), writes=(bplo[sp],))
                        tk.emit(tk.pool, lambda s=sp, hh=h8: nc.gpsimd.tensor_add(
                            spart[:, hh * 512:(hh + 1) * 512],
                            spart[:, hh * 512:(hh + 1) * 512],
                            phi[:, s * 512:(s + 1) * 512].bitcast(F32)),
                            reads=(bphi[sp], bsp), writes=(bsp,))
                        lvh = vhb[:, kc * 512 + kv * 64: kc * 512 + kv * 64 + 64]
                        lvl = vlb[:, kc * 512 + kv * 64: kc * 512 + kv * 64 + 64]
                        tk.emit(tk.pe, lambda a=lvh, s=sp, s0=(kc == 0): nc.tensor.matmul(
                            psAV[0:64, :], a, phi[:, s * 512:(s + 1) * 512],
                            start=s0, stop=False, skip_group_check=True),
                            reads=(bvb, bphi[sp]), writes=(bpsAV,))
                        tk.emit(tk.pe, lambda a=lvh, s=sp: nc.tensor.matmul(
                            psAV[0:64, :], a, plo[:, s * 512:(s + 1) * 512],
                            start=False, stop=False, skip_group_check=True),
                            reads=(bvb, bplo[sp]), writes=(bpsAV,))
                        tk.emit(tk.pe, lambda a=lvl, s=sp, s1=(kc == 3): nc.tensor.matmul(
                            psAV[0:64, :], a, phi[:, s * 512:(s + 1) * 512],
                            start=False, stop=s1, skip_group_check=True),
                            reads=(bvb, bphi[sp]), writes=(bpsAV,))
                    ypos = ysb[(h % 2) * 64:(h % 2) * 64 + 64,
                               (h // 2) * 512:(h // 2) * 512 + 512]
                    tk.emit(tk.dve, lambda y=ypos: nc.vector.tensor_add(
                        y, y, psAV[0:64, :]), reads=(bpsAV, by), writes=(by,))
            for h8 in range(G):
                h = hg * G + h8
                ssl = spart[:, h8 * 512:(h8 + 1) * 512]
                tk.emit(tk.dve, lambda s=ssl: nc.vector.tensor_copy(sphi, s),
                        reads=(bsp,), writes=(bspl,))
                tk.emit(tk.dve, lambda s=ssl: nc.vector.tensor_sub(splo, s, sphi),
                        reads=(bsp, bspl), writes=(bspl,))
                tk.emit(tk.pe, lambda: nc.tensor.matmul(
                    psD[0:1, :], ctx.c_oc128, sphi, start=True, stop=False,
                    skip_group_check=True),
                    reads=(bspl, B("consts")), writes=(bpsD,))
                tk.emit(tk.pe, lambda: nc.tensor.matmul(
                    psD[0:1, :], ctx.c_oc128, splo, start=False, stop=True,
                    skip_group_check=True),
                    reads=(bspl, B("consts")), writes=(bpsD,))
                tk.emit(tk.dve, lambda: nc.vector.reciprocal(r32, psD[0:1, :]),
                        reads=(bpsD,), writes=(brr,))
                tk.emit(tk.dve, lambda: nc.vector.tensor_copy(rhi, r32),
                        reads=(brr,), writes=(brr,))
                tk.emit(tk.dve, lambda: nc.vector.tensor_sub(rlo, r32, rhi),
                        reads=(brr,), writes=(brr,))
                tk.emit(tk.pe, lambda: nc.tensor.matmul(
                    psB, ctx.c_or128, rhi, start=True, stop=False,
                    skip_group_check=True),
                    reads=(brr, B("consts")), writes=(bpsB,))
                tk.emit(tk.pe, lambda: nc.tensor.matmul(
                    psB, ctx.c_or128, rlo, start=False, stop=True,
                    skip_group_check=True),
                    reads=(brr, B("consts")), writes=(bpsB,))
                tk.emit(tk.dve, lambda: nc.vector.tensor_copy(bvw, psB),
                        reads=(bpsB,), writes=(bbv,))
                base = (h % 2) * 64
                ypos = ysb[base:base + 64,
                           (h // 2) * 512:(h // 2) * 512 + 512]
                tk.emit(tk.dve, lambda y=ypos, b0=base: nc.vector.tensor_mul(
                    y, y, bvw[b0:b0 + 64, :]), reads=(bbv, by), writes=(by,))
        esA.close()
        tk.barrier()

        esB = ExitStack()

        def alB(name, shape, dtype, _es=esB, _j=j):
            return _es.enter_context(
                nc.sbuf_tensor(f"{name}B{_j}", shape, dtype)).ap()

        def alpB(name, _es=esB, _j=j):
            return _es.enter_context(
                nc.psum_tensor(f"{name}B{_j}", [128, 512], F32)).ap()

        wob = alB("b_wob", [128, 2048], F32)
        wohi = alB("b_wohi", [128, 2048], F32R)
        wolo = alB("b_wolo", [128, 2048], F32R)
        o32 = alB("b_o32", [128, 512], F32)
        ylo = alB("b_ylo", [128, 16 * 512], F32R)
        x0 = alB("b_x0", [128, 4 * 2048], F32)
        psO = alpB("b_psO")
        psT = [alpB("b_psT0"), alpB("b_psT1")]
        bwo = B("b_wo")
        bwos = B("b_wos")
        bwos2 = B("b_wos2")
        bo32 = B("b_o32")
        bx0 = B("b_x0")
        bpsO = B("b_psO")
        bpsT = [B("b_psT0"), B("b_psT1")]

        tk.emit(tk.dve, lambda: nc.vector.tensor_copy(yhi, ysb),
                reads=(by,), writes=(byhi,))
        tk.emit(tk.dve, lambda y=ylo: nc.vector.tensor_sub(y, ysb, yhi),
                reads=(by, byhi), writes=(by,))
        tk.emit(tk.gd, lambda jj=j: nc.gpsimd.dma_start(
            out=x0.rearrange("p (s d) -> p s d", s=4),
            in_=t["xown"][jj * 512:(jj + 1) * 512, :].rearrange(
                "(s p) d -> p s d", p=128)), writes=(bx0,))
        for dc in range(16):
            tk.emit(tk.sd, lambda d=dc: nc.sync.dma_start(
                out=wob, in_=t["wo_t"][d]), writes=(bwo,))
            emit_wsplit(ctx, tk.pool, wob, bwo, wohi, bwos, wolo, bwos2)
            for c in range(16):
                lh = wohi[:, c * 128:(c + 1) * 128]
                ll = wolo[:, c * 128:(c + 1) * 128]
                rh = yhi[:, c * 512:(c + 1) * 512]
                rl = ylo[:, c * 512:(c + 1) * 512]
                tk.emit(tk.pe, lambda a=lh, b_=rh, s=(c == 0): nc.tensor.matmul(
                    psO, a, b_, start=s, stop=False, skip_group_check=True),
                    reads=(bwos, byhi), writes=(bpsO,))
                tk.emit(tk.pe, lambda a=lh, b_=rl: nc.tensor.matmul(
                    psO, a, b_, start=False, stop=False, skip_group_check=True),
                    reads=(bwos, by), writes=(bpsO,))
                tk.emit(tk.pe, lambda a=ll, b_=rh, s=(c == 15): nc.tensor.matmul(
                    psO, a, b_, start=False, stop=s, skip_group_check=True),
                    reads=(bwos2, byhi), writes=(bpsO,))
            tk.emit(tk.dve, lambda: nc.vector.tensor_copy(o32, psO),
                    reads=(bpsO,), writes=(bo32,))
            for ts in range(4):
                alt = ts % 2
                tk.emit(tk.pe, lambda s=ts, a=alt: nc.tensor.transpose(
                    psT[a][:, 0:128], o32[:, s * 128:(s + 1) * 128], ctx.c_ident),
                    reads=(bo32, B("consts")), writes=(bpsT[alt],))
                dst = x0[:, ts * 2048 + dc * 128: ts * 2048 + dc * 128 + 128]
                tk.emit(tk.dve, lambda d=dst, a=alt: nc.vector.tensor_add(
                    d, d, psT[a][:, 0:128]), reads=(bpsT[alt], bx0), writes=(bx0,))
        tk.emit(tk.gd, lambda jj=j: nc.gpsimd.dma_start(
            out=t["x1_hbm"][jj * 512:(jj + 1) * 512, :].rearrange(
                "(s p) d -> p s d", p=128),
            in_=x0.rearrange("p (s d) -> p s d", s=4)),
            reads=(bx0,), writes=(bX1,))
        esB.close()
        tk.barrier()
    ctx.es_attn.close()


def emit_phase1(ctx):
    nc, tk, t = ctx.nc, ctx.tk, ctx.t
    B = ctx.buf
    for j in range(4):
        es = ExitStack()

        def al(name, shape, dtype, _es=es, _j=j):
            return _es.enter_context(nc.sbuf_tensor(f"f{name}_{_j}", shape, dtype)).ap()

        def alp(name, _es=es, _j=j):
            return _es.enter_context(nc.psum_tensor(f"f{name}_{_j}", [128, 512], F32)).ap()

        xa = al("xa", [128, 4 * 2048], F32)
        xsq = al("xsq", [128, 2048], F32)
        stats = al("stats", [128, 32], F32)
        xr = al("xr", [128, 16 * 512], F32R)
        wb = al("wb", [128, 2 * 2048], F32)
        wh = al("wh", [128, 2 * 2048], F32R)
        wl = al("wl", [128, 2 * 2048], F32R)
        dr = al("dr", [128, 16 * 512], F32R)
        b0s = al("b0s", [128, 64], F32)
        b1s = al("b1s", [128, 16], F32)
        ei = al("ei", [128, 2 * 512], F32)
        esg = al("esg", [128, 2 * 512], F32)
        xres = al("xres", [128, 4 * 2048], F32)
        psD = [alp("psD0"), alp("psD1")]
        psE = [alp("psE0"), alp("psE1")]
        psT = [alp("psT0"), alp("psT1")]

        bxa, bxsq, bst = B("f_xa"), B("f_xsq"), B("f_st")
        bxr = B("f_xr")
        bw = [B("f_w0"), B("f_w1")]
        bwh = [B("f_wh0"), B("f_wh1")]
        bwl = [B("f_wl0"), B("f_wl1")]
        bdr = B("f_dr")
        bb = B("f_bias")
        bei = [B("f_ei0"), B("f_ei1")]
        bes = [B("f_es0"), B("f_es1")]
        bxres = B("f_xres")
        bpsD = [B("f_psD0"), B("f_psD1")]
        bpsE = [B("f_psE0"), B("f_psE1")]
        bpsT = [B("f_psT0"), B("f_psT1")]
        bE = B("h_e")
        bX2 = B("h_x2")

        tk.emit(tk.sd, lambda jj=j: nc.sync.dma_start(
            out=b0s, in_=t["b0m"]), writes=(bb,))
        tk.emit(tk.sd, lambda jj=j: nc.sync.dma_start(
            out=b1s, in_=t["b1m"]), writes=(bb,))
        tk.emit(tk.gd, lambda jj=j: nc.gpsimd.dma_start(
            out=xa.rearrange("p (s d) -> p s d", s=4),
            in_=t["x1_hbm"][jj * 512:(jj + 1) * 512, :].rearrange(
                "(s p) d -> p s d", p=128)), reads=(B("h_x1"),), writes=(bxa,))
        for sub in range(4):
            emit_ln(ctx, xa, bxa, xsq, bxsq, stats, bst, sub)
        emit_transpose_split(ctx, xa, bxa, psT, bpsT, (xr,), (bxr,), split=False)
        for ns in range(4):
            for ncs in range(16):
                nchunk = ns * 16 + ncs
                sl = ncs % 2
                wslice = wb[:, sl * 2048: sl * 2048 + 2048]
                whs = wh[:, sl * 2048: sl * 2048 + 2048]
                wls = wl[:, sl * 2048: sl * 2048 + 2048]
                tk.emit(tk.sd, lambda n=nchunk, w=wslice: nc.sync.dma_start(
                    out=w, in_=t["w0_t"][n]), writes=(bw[sl],))
                tk.emit(tk.pool, lambda a=whs, w=wslice: nc.gpsimd.tensor_copy(a, w),
                        reads=(bw[sl],), writes=(bwh[sl],))
                tk.emit(tk.pool, lambda a=wls, w=wslice, h=whs: nc.gpsimd.tensor_sub(
                    a, w, h), reads=(bw[sl], bwh[sl]), writes=(bwl[sl],))
                pp = ncs % 2
                for c in range(16):
                    lh = wh[:, sl * 2048 + c * 128: sl * 2048 + c * 128 + 128]
                    ll = wl[:, sl * 2048 + c * 128: sl * 2048 + c * 128 + 128]
                    rr = xr[:, c * 512:(c + 1) * 512]
                    tk.emit(tk.pe, lambda a=lh, b_=rr, s=(c == 0), p=pp: nc.tensor.matmul(
                        psD[p], a, b_, start=s, stop=False, skip_group_check=True),
                        reads=(bwh[sl], bxr), writes=(bpsD[pp],))
                    tk.emit(tk.pe, lambda a=ll, b_=rr, s=(c == 15), p=pp: nc.tensor.matmul(
                        psD[p], a, b_, start=False, stop=s, skip_group_check=True),
                        reads=(bwl[sl], bxr), writes=(bpsD[pp],))
                tk.emit(tk.act, lambda n=nchunk, cs=ncs, p=pp: nc.scalar.activation(
                    out=dr[:, cs * 512:(cs + 1) * 512], in_=psD[p], func=AF.Gelu,
                    bias=b0s[:, n:n + 1]), reads=(bpsD[pp], bb), writes=(bdr,))
            if ns == 3:
                tk.emit(tk.gd, lambda jj=j: nc.gpsimd.dma_start(
                    out=xres.rearrange("p (s d) -> p s d", s=4),
                    in_=t["x1_hbm"][jj * 512:(jj + 1) * 512, :].rearrange(
                        "(s p) d -> p s d", p=128)), reads=(B("h_x1"),), writes=(bxres,))
            for hc in range(16):
                sl = hc % 2
                wslice = wb[:, sl * 2048: sl * 2048 + 2048]
                whs = wh[:, sl * 2048: sl * 2048 + 2048]
                wls = wl[:, sl * 2048: sl * 2048 + 2048]
                tk.emit(tk.sd, lambda h=hc, n=ns, w=wslice: nc.sync.dma_start(
                    out=w, in_=t["w1_t"][h][:, n * 2048:(n + 1) * 2048]),
                    writes=(bw[sl],))
                tk.emit(tk.pool, lambda a=whs, w=wslice: nc.gpsimd.tensor_copy(a, w),
                        reads=(bw[sl],), writes=(bwh[sl],))
                tk.emit(tk.pool, lambda a=wls, w=wslice, h=whs: nc.gpsimd.tensor_sub(
                    a, w, h), reads=(bw[sl], bwh[sl]), writes=(bwl[sl],))
                pp = hc % 2
                for ic in range(16):
                    lh = wh[:, sl * 2048 + ic * 128: sl * 2048 + ic * 128 + 128]
                    ll = wl[:, sl * 2048 + ic * 128: sl * 2048 + ic * 128 + 128]
                    rr = dr[:, ic * 512:(ic + 1) * 512]
                    tk.emit(tk.pe, lambda a=lh, b_=rr, s=(ic == 0), p=pp: nc.tensor.matmul(
                        psE[p], a, b_, start=s, stop=False, skip_group_check=True),
                        reads=(bwh[sl], bdr), writes=(bpsE[pp],))
                    tk.emit(tk.pe, lambda a=ll, b_=rr, s=(ic == 15), p=pp: nc.tensor.matmul(
                        psE[p], a, b_, start=False, stop=s, skip_group_check=True),
                        reads=(bwl[sl], bdr), writes=(bpsE[pp],))
                esl = esg[:, sl * 512: sl * 512 + 512]
                eis = ei[:, sl * 512: sl * 512 + 512]
                ecol = lambda h, jj: slice(h * 2048 + jj * 512, h * 2048 + jj * 512 + 512)
                if ns == 0:
                    tk.emit(tk.dve, lambda d=esl, p=pp: nc.vector.tensor_copy(
                        d, psE[p]), reads=(bpsE[pp],), writes=(bes[sl],))
                    tk.emit(tk.gd, lambda s=esl, h=hc, jj=j: nc.gpsimd.dma_start(
                        out=t["e_hbm"][:, ecol(h, jj)], in_=s),
                        reads=(bes[sl],), writes=(bE,))
                elif ns < 3:
                    tk.emit(tk.sd, lambda d=eis, h=hc, jj=j: nc.sync.dma_start(
                        out=d, in_=t["e_hbm"][:, ecol(h, jj)]),
                        reads=(bE,), writes=(bei[sl],))
                    tk.emit(tk.dve, lambda d=esl, i_=eis, p=pp: nc.vector.tensor_add(
                        d, i_, psE[p]), reads=(bei[sl], bpsE[pp]), writes=(bes[sl],))
                    tk.emit(tk.gd, lambda s=esl, h=hc, jj=j: nc.gpsimd.dma_start(
                        out=t["e_hbm"][:, ecol(h, jj)], in_=s),
                        reads=(bes[sl],), writes=(bE,))
                else:
                    tk.emit(tk.sd, lambda d=eis, h=hc, jj=j: nc.sync.dma_start(
                        out=d, in_=t["e_hbm"][:, ecol(h, jj)]),
                        reads=(bE,), writes=(bei[sl],))
                    tk.emit(tk.dve, lambda d=esl, i_=eis, h=hc, p=pp:
                            nc.vector.scalar_tensor_tensor(
                                out=d, in0=psE[p], scalar=b1s[:, h:h + 1], in1=i_,
                                op0=ALU.add, op1=ALU.add),
                            reads=(bei[sl], bpsE[pp], bb), writes=(bes[sl],))
                    for ts in range(4):
                        alt = ts % 2
                        tk.emit(tk.pe, lambda s=esl, w=ts, a=alt: nc.tensor.transpose(
                            psT[a][:, 0:128], s[:, w * 128:(w + 1) * 128], ctx.c_ident),
                            reads=(bes[sl], B("consts")), writes=(bpsT[alt],))
                        dst = xres[:, ts * 2048 + hc * 128: ts * 2048 + hc * 128 + 128]
                        tk.emit(tk.dve, lambda d=dst, a=alt: nc.vector.tensor_add(
                            d, d, psT[a][:, 0:128]),
                            reads=(bpsT[alt], bxres), writes=(bxres,))
        tk.emit(tk.gd, lambda jj=j: nc.gpsimd.dma_start(
            out=t["x2_hbm"][jj * 512:(jj + 1) * 512, :].rearrange(
                "(s p) d -> p s d", p=128),
            in_=xres.rearrange("p (s d) -> p s d", s=4)),
            reads=(bxres,), writes=(bX2,))
        es.close()
        tk.barrier()


def emit_phase2(ctx):
    nc, tk, t = ctx.nc, ctx.tk, ctx.t
    B = ctx.buf
    EPS = 1e-6

    es2 = ExitStack()
    kv_sb = es2.enter_context(nc.sbuf_tensor("l_kv", [64, 16 * 132], F32)).ap()
    kvh = es2.enter_context(nc.sbuf_tensor("l_kvh", [64, 16 * 132], F32R)).ap()
    kvl = es2.enter_context(nc.sbuf_tensor("l_kvl", [64, 16 * 132], F32R)).ap()
    mqh = es2.enter_context(nc.sbuf_tensor("l_mqh", [128, 64], F32R)).ap()
    mql = es2.enter_context(nc.sbuf_tensor("l_mql", [128, 64], F32R)).ap()
    mkh = es2.enter_context(nc.sbuf_tensor("l_mkh", [128, 64], F32R)).ap()
    mkl = es2.enter_context(nc.sbuf_tensor("l_mkl", [128, 64], F32R)).ap()
    mstage = es2.enter_context(nc.sbuf_tensor("l_mst", [128, 2 * 64], F32)).ap()
    bkv = B("l_kv")
    bkvhl = B("l_kvhl")
    bmap = B("l_map")
    bmst = B("l_mst")

    tk.emit(tk.sd, lambda: nc.sync.dma_start(
        out=mstage[:, 0:64], in_=t["mapq_T"]), writes=(bmst,))
    tk.emit(tk.sd, lambda: nc.sync.dma_start(
        out=mstage[:, 64:128], in_=t["mapk_T"]), writes=(bmst,))
    tk.emit(tk.dve, lambda: nc.vector.tensor_copy(mqh, mstage[:, 0:64]),
            reads=(bmst,), writes=(bmap,))
    tk.emit(tk.dve, lambda: nc.vector.tensor_sub(mql, mstage[:, 0:64], mqh),
            reads=(bmst,), writes=(bmap,))
    tk.emit(tk.dve, lambda: nc.vector.tensor_copy(mkh, mstage[:, 64:128]),
            reads=(bmst,), writes=(bmap,))
    tk.emit(tk.dve, lambda: nc.vector.tensor_sub(mkl, mstage[:, 64:128], mkh),
            reads=(bmst,), writes=(bmap,))
    tk.emit(tk.dve, lambda: nc.vector.memset(kv_sb, 0.0), writes=(bkv,))

    for j in range(4):
        es = ExitStack()

        def al(name, shape, dtype, _es=es, _j=j):
            return _es.enter_context(nc.sbuf_tensor(f"l{name}_{_j}", shape, dtype)).ap()

        def alp(name, _es=es, _j=j):
            return _es.enter_context(nc.psum_tensor(f"l{name}_{_j}", [128, 512], F32)).ap()

        xnfh = al("xnfh", [128, 16 * 512], F32R)
        xnfl = al("xnfl", [128, 16 * 512], F32R)
        wb = al("wb", [128, 2 * 2048], F32)
        wh = al("wh", [128, 2048], F32R)
        wl = al("wl", [128, 2048], F32R)
        qrh = al("qrh", [128, 2 * 512], F32R)
        qrl = al("qrl", [128, 2 * 512], F32R)
        phq = al("phq", [64, 2 * 512], F32)
        elt = al("elt", [128, 2 * 512], F32)
        ele = al("ele", [128, 2 * 512], F32)
        fkst = al("fkst", [128, 2 * 64], F32)
        fkh = al("fkh", [128, 4 * 1024], F32R)
        fkl = al("fkl", [128, 4 * 1024], F32R)
        vstg = al("vstg", [128, 512], F32)
        vT = al("vT", [128, 2 * 4 * 132], F32R)
        bias_sb = al("bias", [128, 3 * 16], F32)
        psA = [alp("psA0"), alp("psA1")]
        psM = alp("psM")
        psT = [alp("psT0"), alp("psT1")]
        psKV = alp("psKV")
        esLN = ExitStack()
        xa = esLN.enter_context(
            nc.sbuf_tensor(f"lxa_{j}", [128, 2 * 2048], F32)).ap()
        xsq = esLN.enter_context(
            nc.sbuf_tensor(f"lxsq_{j}", [128, 2048], F32)).ap()
        stats = esLN.enter_context(
            nc.sbuf_tensor(f"lstats_{j}", [128, 32], F32)).ap()

        bxa, bxsq, bst = B("l_xa"), B("l_xsq"), B("l_st")
        bxnfh, bxnfl = B("l_xnfh"), B("l_xnfl")
        bw = [B("l_w0"), B("l_w1")]
        bwh = [B("l_wh0"), B("l_wh1")]
        bwl = [B("l_wl0"), B("l_wl1")]
        bqr = [B("l_qr0"), B("l_qr1")]
        bphq = [B("l_phq0"), B("l_phq1")]
        bel = [B("l_el0"), B("l_el1")]
        bfkT = B("l_fkT")
        bfkhl = B("l_fkhl")
        bvstg = B("l_vstg")
        bvT = [B("l_vT0"), B("l_vT1")]
        bbias = B("l_bias")
        bpsA = [B("l_psA0"), B("l_psA1")]
        bpsM = B("l_psM")
        bpsT = [B("l_psT0"), B("l_psT1")]
        bpsKV = B("l_psKV")
        bPH = B("h_phiq")

        tk.emit(tk.sd, lambda: nc.sync.dma_start(
            out=bias_sb[:, 0:16], in_=t["l1bqm"]), writes=(bbias,))
        tk.emit(tk.sd, lambda: nc.sync.dma_start(
            out=bias_sb[:, 16:32], in_=t["l1bkm"]), writes=(bbias,))
        for half in range(2):
            tk.emit(tk.gd, lambda jj=j, h=half: nc.gpsimd.dma_start(
                out=xa.rearrange("p (s d) -> p s d", s=2),
                in_=t["x2_hbm"][jj * 512 + h * 256: jj * 512 + h * 256 + 256,
                                :].rearrange("(s p) d -> p s d", p=128)),
                reads=(B("h_x2"),), writes=(bxa,))
            for s2 in range(2):
                emit_ln(ctx, xa, bxa, xsq, bxsq, stats, bst, s2)
            emit_transpose_split(ctx, xa, bxa, psT, bpsT,
                                 (xnfh, xnfl), (bxnfh, bxnfl), split=True,
                                 subs=[(0, half * 2), (1, half * 2 + 1)])
        esLN.close()
        tk.barrier()
        for sl in range(2):
            for ts in range(4):
                tk.emit(tk.dve, lambda s=sl, w=ts: nc.vector.tensor_copy(
                    vT[:, s * 528 + w * 132 + 128: s * 528 + w * 132 + 129],
                    ctx.c_oc128), reads=(B("consts"),), writes=(bvT[sl],))
                for zc in range(3):
                    tk.emit(tk.dve, lambda s=sl, w=ts, z=zc: nc.vector.tensor_scalar_mul(
                        vT[:, s * 528 + w * 132 + 129 + z:
                           s * 528 + w * 132 + 130 + z], ctx.c_oc128.bitcast(F32), 0.0),
                        reads=(B("consts"),), writes=(bvT[sl],))

        def proj_chunk(wname, hc, pp, sl, bias_col):
            wslice = wb[:, sl * 2048: sl * 2048 + 2048]
            whs = wh
            wls = wl
            tk.emit(tk.sd, lambda n=hc, w=wslice, nm=wname: nc.sync.dma_start(
                out=w, in_=t[nm][n]), writes=(bw[sl],))
            tk.emit(tk.pool, lambda a=whs, w=wslice: nc.gpsimd.tensor_copy(a, w),
                    reads=(bw[sl],), writes=(bwh[0],))
            tk.emit(tk.pool, lambda a=wls, w=wslice, h=whs: nc.gpsimd.tensor_sub(
                a, w, h), reads=(bw[sl], bwh[0]), writes=(bwl[0],))
            for c in range(16):
                lh = wh[:, c * 128: c * 128 + 128]
                ll = wl[:, c * 128: c * 128 + 128]
                rh = xnfh[:, c * 512:(c + 1) * 512]
                rl = xnfl[:, c * 512:(c + 1) * 512]
                tk.emit(tk.pe, lambda a=lh, b_=rh, s=(c == 0), p=pp: nc.tensor.matmul(
                    psA[p], a, b_, start=s, stop=False, skip_group_check=True),
                    reads=(bwh[0], bxnfh), writes=(bpsA[pp],))
                tk.emit(tk.pe, lambda a=lh, b_=rl, p=pp: nc.tensor.matmul(
                    psA[p], a, b_, start=False, stop=False, skip_group_check=True),
                    reads=(bwh[0], bxnfl), writes=(bpsA[pp],))
                tk.emit(tk.pe, lambda a=ll, b_=rh, s=(c == 15), p=pp: nc.tensor.matmul(
                    psA[p], a, b_, start=False, stop=s, skip_group_check=True),
                    reads=(bwl[0], bxnfh), writes=(bpsA[pp],))

        for hc in range(16):
            pp = hc % 2
            sl = hc % 2
            proj_chunk("l1wq_t", hc, pp, sl, hc)
            qh = qrh[:, sl * 512: sl * 512 + 512]
            ql = qrl[:, sl * 512: sl * 512 + 512]
            tk.emit(tk.act, lambda d=qh, h=hc, p=pp: nc.scalar.activation(
                out=d, in_=psA[p], func=AF.Identity, bias=bias_sb[:, h:h + 1]),
                reads=(bpsA[pp], bbias), writes=(bqr[sl],))
            tk.emit(tk.dve, lambda d=ql, h=hc, p=pp, hh=qh:
                    nc.vector.scalar_tensor_tensor(
                        out=d, in0=psA[p], scalar=bias_sb[:, h:h + 1], in1=hh,
                        op0=ALU.add, op1=ALU.subtract),
                    reads=(bpsA[pp], bbias, bqr[sl]), writes=(bqr[sl],))
            tk.emit(tk.pe, lambda a=qh: nc.tensor.matmul(
                psM[0:64, :], mqh, a, start=True, stop=False, skip_group_check=True),
                reads=(bmap, bqr[sl]), writes=(bpsM,))
            tk.emit(tk.pe, lambda a=ql: nc.tensor.matmul(
                psM[0:64, :], mqh, a, start=False, stop=False, skip_group_check=True),
                reads=(bmap, bqr[sl]), writes=(bpsM,))
            tk.emit(tk.pe, lambda a=qh: nc.tensor.matmul(
                psM[0:64, :], mql, a, start=False, stop=True, skip_group_check=True),
                reads=(bmap, bqr[sl]), writes=(bpsM,))
            t1 = elt[0:64, sl * 512: sl * 512 + 512]
            e1 = ele[0:64, sl * 512: sl * 512 + 512]
            tk.emit(tk.dve, lambda d=t1: nc.vector.tensor_scalar_min(
                d, psM[0:64, :], 0.0), reads=(bpsM,), writes=(bel[sl],))
            tk.emit(tk.act, lambda d=e1, s=t1: nc.scalar.activation(
                out=d, in_=s, func=AF.Exp), reads=(bel[sl],), writes=(bel[sl],))
            tk.emit(tk.dve, lambda d=t1: nc.vector.tensor_scalar_max(
                d, psM[0:64, :], 0.0), reads=(bpsM, bel[sl]), writes=(bel[sl],))
            dstq = phq[:, sl * 512:(sl + 1) * 512]
            tk.emit(tk.dve, lambda d=dstq, a=e1, b_=t1: nc.vector.tensor_add(
                d, a, b_), reads=(bel[sl],), writes=(bphq[sl],))
            tk.emit(tk.gd, lambda s=dstq, h=hc, jj=j: nc.gpsimd.dma_start(
                out=t["phiq_hbm"][:, h * 2048 + jj * 512: h * 2048 + jj * 512 + 512],
                in_=s), reads=(bphq[sl],), writes=(bPH,))
        for hc in range(16):
            pp = hc % 2
            sl = hc % 2
            proj_chunk("l1wk_t", hc, pp, sl, 16 + hc)
            kh = qrh[:, sl * 512: sl * 512 + 512]
            kl = qrl[:, sl * 512: sl * 512 + 512]
            tk.emit(tk.act, lambda d=kh, h=hc, p=pp: nc.scalar.activation(
                out=d, in_=psA[p], func=AF.Identity, bias=bias_sb[:, 16 + h:17 + h]),
                reads=(bpsA[pp], bbias), writes=(bqr[sl],))
            tk.emit(tk.dve, lambda d=kl, h=hc, p=pp, hh=kh:
                    nc.vector.scalar_tensor_tensor(
                        out=d, in0=psA[p], scalar=bias_sb[:, 16 + h:17 + h], in1=hh,
                        op0=ALU.add, op1=ALU.subtract),
                    reads=(bpsA[pp], bbias, bqr[sl]), writes=(bqr[sl],))
            for ts in range(4):
                lh = qrh[:, sl * 512 + ts * 128: sl * 512 + ts * 128 + 128]
                ll = qrl[:, sl * 512 + ts * 128: sl * 512 + ts * 128 + 128]
                tk.emit(tk.pe, lambda a=lh: nc.tensor.matmul(
                    psM[:, 0:64], a, mkh, start=True, stop=False,
                    skip_group_check=True),
                    reads=(bmap, bqr[sl]), writes=(bpsM,))
                tk.emit(tk.pe, lambda a=ll: nc.tensor.matmul(
                    psM[:, 0:64], a, mkh, start=False, stop=False,
                    skip_group_check=True),
                    reads=(bmap, bqr[sl]), writes=(bpsM,))
                tk.emit(tk.pe, lambda a=lh: nc.tensor.matmul(
                    psM[:, 0:64], a, mkl, start=False, stop=True,
                    skip_group_check=True),
                    reads=(bmap, bqr[sl]), writes=(bpsM,))
                t1 = elt[:, sl * 512: sl * 512 + 64]
                e1 = ele[:, sl * 512: sl * 512 + 64]
                tk.emit(tk.dve, lambda d=t1: nc.vector.tensor_scalar_min(
                    d, psM[:, 0:64], 0.0), reads=(bpsM,), writes=(bel[sl],))
                tk.emit(tk.act, lambda d=e1, s=t1: nc.scalar.activation(
                    out=d, in_=s, func=AF.Exp), reads=(bel[sl],), writes=(bel[sl],))
                tk.emit(tk.dve, lambda d=t1: nc.vector.tensor_scalar_max(
                    d, psM[:, 0:64], 0.0), reads=(bpsM, bel[sl]), writes=(bel[sl],))
                dstk = fkst[:, sl * 64: sl * 64 + 64]
                tk.emit(tk.dve, lambda d=dstk, a=e1, b_=t1: nc.vector.tensor_add(
                    d, a, b_), reads=(bel[sl],), writes=(bfkT,))
                fh = fkh[:, ts * 1024 + hc * 64: ts * 1024 + hc * 64 + 64]
                fl = fkl[:, ts * 1024 + hc * 64: ts * 1024 + hc * 64 + 64]
                tk.emit(tk.dve, lambda d=fh, s=dstk: nc.vector.tensor_copy(d, s),
                        reads=(bfkT,), writes=(bfkhl,))
                tk.emit(tk.dve, lambda d=fl, s=dstk, hh=fh: nc.vector.tensor_sub(
                    d, s, hh), reads=(bfkT,), writes=(bfkhl,))
        for hc in range(16):
            pp = hc % 2
            sl = hc % 2
            proj_chunk("l1wv_t", hc, pp, sl, None)
            tk.emit(tk.dve, lambda p=pp: nc.vector.tensor_copy(vstg, psA[p]),
                    reads=(bpsA[pp],), writes=(bvstg,))
            for ts in range(4):
                alt = ts % 2
                tk.emit(tk.pe, lambda w=ts, a=alt: nc.tensor.transpose(
                    psT[a][:, 0:128], vstg[:, w * 128:(w + 1) * 128], ctx.c_ident),
                    reads=(bvstg, B("consts")), writes=(bpsT[alt],))
                vh = vT[:, sl * 528 + ts * 132: sl * 528 + ts * 132 + 128]
                tk.emit(tk.dve, lambda d=vh, a=alt: nc.vector.tensor_copy(
                    d, psT[a][:, 0:128]), reads=(bpsT[alt],), writes=(bvT[sl],))
            for ts in range(4):
                lh = fkh[:, ts * 1024 + hc * 64: ts * 1024 + hc * 64 + 64]
                ll = fkl[:, ts * 1024 + hc * 64: ts * 1024 + hc * 64 + 64]
                rhs = vT[:, sl * 528 + ts * 132: sl * 528 + ts * 132 + 132]
                tk.emit(tk.pe, lambda a=lh, b_=rhs, s=(ts == 0): nc.tensor.matmul(
                    psKV[0:64, 0:132], a, b_, start=s, stop=False,
                    skip_group_check=True),
                    reads=(bfkhl, bvT[sl]), writes=(bpsKV,))
                tk.emit(tk.pe, lambda a=lh, b_=rhs: nc.tensor.matmul(
                    psKV[0:64, 0:132], a, b_, start=False, stop=False,
                    skip_group_check=True),
                    reads=(bfkhl, bvT[sl]), writes=(bpsKV,))
                tk.emit(tk.pe, lambda a=ll, b_=rhs, s=(ts == 3): nc.tensor.matmul(
                    psKV[0:64, 0:132], a, b_, start=False, stop=s,
                    skip_group_check=True),
                    reads=(bfkhl, bvT[sl]), writes=(bpsKV,))
            dkv = kv_sb[:, hc * 132:(hc + 1) * 132]
            tk.emit(tk.dve, lambda d=dkv: nc.vector.tensor_add(
                d, d, psKV[0:64, 0:132]), reads=(bpsKV, bkv), writes=(bkv,))
        es.close()
        tk.barrier()

    bKIO = B("h_kvio")
    tk.emit(tk.gd, lambda: nc.gpsimd.dma_start(
        out=t["kv_in"], in_=kv_sb), reads=(bkv,), writes=(bKIO,))
    tk.emit(tk.pool, lambda: nc.gpsimd.collective_compute(
        "AllReduce", ALU.add,
        replica_groups=[[0, 1], [2, 3], [4, 5], [6, 7]],
        ins=[t["kv_in"]], outs=[t["kv_out"]]),
        reads=(bKIO,), writes=(bKIO,))
    tk.emit(tk.gd, lambda: nc.gpsimd.dma_start(
        out=kv_sb, in_=t["kv_out"]), reads=(bKIO,), writes=(bkv,))
    tk.emit(tk.dve, lambda: nc.vector.tensor_copy(kvh, kv_sb),
            reads=(bkv,), writes=(bkvhl,))
    tk.emit(tk.dve, lambda: nc.vector.tensor_sub(kvl, kv_sb, kvh),
            reads=(bkv,), writes=(bkvhl,))
    tk.barrier()

    for j in range(4):
        es = ExitStack()

        def al(name, shape, dtype, _es=es, _j=j):
            return _es.enter_context(nc.sbuf_tensor(f"n{name}_{_j}", shape, dtype)).ap()

        def alp(name, _es=es, _j=j):
            return _es.enter_context(nc.psum_tensor(f"n{name}_{_j}", [128, 512], F32)).ap()

        phq = al("phq", [64, 2 * 512], F32)
        fqh = al("fqh", [64, 2 * 512], F32R)
        fql = al("fql", [64, 2 * 512], F32R)
        den = al("den", [1, 512], F32)
        r32 = al("r32", [1, 512], F32)
        rhi = al("rhi", [1, 512], F32R)
        rlo = al("rlo", [1, 512], F32R)
        bvw = al("bvw", [128, 512], F32)
        b2w = al("b2w", [128, 512], F32)
        ysb = al("ysb", [128, 16 * 512], F32)
        yhi = al("yhi", [128, 16 * 512], F32R)
        ylo = al("ylo", [128, 16 * 512], F32R)
        wb = al("wb", [128, 2048], F32)
        wh = al("wh", [128, 2048], F32R)
        wl = al("wl", [128, 2048], F32R)
        o32 = al("o32", [128, 512], F32)
        xres = al("xres", [128, 4 * 2048], F32)
        bias_sb = al("bias", [128, 2 * 16], F32)
        psN = alp("psN")
        psD = alp("psD")
        psB = alp("psB")
        psO = alp("psO")
        psT = [alp("psT0"), alp("psT1")]

        bphq = [B("n_phq0"), B("n_phq1")]
        bfq = [B("n_fq0"), B("n_fq1")]
        bden = B("n_den")
        brr = B("n_r")
        bbv = B("n_bv")
        by = B("n_y")
        byhl = B("n_yhl")
        bwo = B("n_wo")
        bwh_ = B("n_wh")
        bwl_ = B("n_wl")
        bo32 = B("n_o32")
        bxres = B("n_xres")
        bbias = B("n_bias")
        bpsN = B("n_psN")
        bpsD = B("n_psD")
        bpsB = B("n_psB")
        bpsO = B("n_psO")
        bpsT = [B("n_psT0"), B("n_psT1")]
        bX3 = B("h_x3")

        tk.emit(tk.sd, lambda: nc.sync.dma_start(
            out=bias_sb[:, 0:16], in_=t["l1bom"]), writes=(bbias,))
        tk.emit(tk.sd, lambda: nc.sync.dma_start(
            out=bias_sb[:, 16:32], in_=t["l1bvm"]), writes=(bbias,))

        tk.emit(tk.gd, lambda jj=j: nc.gpsimd.dma_start(
            out=xres.rearrange("p (s d) -> p s d", s=4),
            in_=t["x2_hbm"][jj * 512:(jj + 1) * 512, :].rearrange(
                "(s p) d -> p s d", p=128)), reads=(B("h_x2"),), writes=(bxres,))
        for h in range(16):
            sl = h % 2
            qh = fqh[:, sl * 512: sl * 512 + 512]
            ql = fql[:, sl * 512: sl * 512 + 512]
            src = phq[:, sl * 512: sl * 512 + 512]
            tk.emit(tk.sd, lambda d=src, h_=h, jj=j: nc.sync.dma_start(
                out=d, in_=t["phiq_hbm"][:, h_ * 2048 + jj * 512:
                                         h_ * 2048 + jj * 512 + 512]),
                reads=(B("h_phiq"),), writes=(bphq[sl],))
            tk.emit(tk.dve, lambda d=qh, s=src: nc.vector.tensor_copy(d, s),
                    reads=(bphq[sl],), writes=(bfq[sl],))
            tk.emit(tk.dve, lambda d=ql, s=src, hh=qh: nc.vector.tensor_sub(
                d, s, hh), reads=(bphq[sl], bfq[sl]), writes=(bfq[sl],))
            lkh = kvh[:, h * 132: h * 132 + 128]
            lkl = kvl[:, h * 132: h * 132 + 128]
            tk.emit(tk.pe, lambda a=lkh, b_=qh: nc.tensor.matmul(
                psN, a, b_, start=True, stop=False, skip_group_check=True),
                reads=(B("l_kvhl"), bfq[sl]), writes=(bpsN,))
            tk.emit(tk.pe, lambda a=lkh, b_=ql: nc.tensor.matmul(
                psN, a, b_, start=False, stop=False, skip_group_check=True),
                reads=(B("l_kvhl"), bfq[sl]), writes=(bpsN,))
            tk.emit(tk.pe, lambda a=lkl, b_=qh: nc.tensor.matmul(
                psN, a, b_, start=False, stop=True, skip_group_check=True),
                reads=(B("l_kvhl"), bfq[sl]), writes=(bpsN,))
            skh = kvh[:, h * 132 + 128: h * 132 + 129]
            skl = kvl[:, h * 132 + 128: h * 132 + 129]
            tk.emit(tk.pe, lambda a=skh, b_=qh: nc.tensor.matmul(
                psD[0:1, :], a, b_, start=True, stop=False, skip_group_check=True),
                reads=(B("l_kvhl"), bfq[sl]), writes=(bpsD,))
            tk.emit(tk.pe, lambda a=skh, b_=ql: nc.tensor.matmul(
                psD[0:1, :], a, b_, start=False, stop=False, skip_group_check=True),
                reads=(B("l_kvhl"), bfq[sl]), writes=(bpsD,))
            tk.emit(tk.pe, lambda a=skl, b_=qh: nc.tensor.matmul(
                psD[0:1, :], a, b_, start=False, stop=True, skip_group_check=True),
                reads=(B("l_kvhl"), bfq[sl]), writes=(bpsD,))
            tk.emit(tk.dve, lambda: nc.vector.tensor_scalar_add(
                den, psD[0:1, :], EPS), reads=(bpsD,), writes=(bden,))
            tk.emit(tk.dve, lambda: nc.vector.reciprocal(r32, den),
                    reads=(bden,), writes=(brr,))
            tk.emit(tk.dve, lambda: nc.vector.tensor_copy(rhi, r32),
                    reads=(brr,), writes=(brr,))
            tk.emit(tk.dve, lambda: nc.vector.tensor_sub(rlo, r32, rhi),
                    reads=(brr,), writes=(brr,))
            tk.emit(tk.pe, lambda: nc.tensor.matmul(
                psB, ctx.c_or128, rhi, start=True, stop=False, skip_group_check=True),
                reads=(brr, B("consts")), writes=(bpsB,))
            tk.emit(tk.pe, lambda: nc.tensor.matmul(
                psB, ctx.c_or128, rlo, start=False, stop=True, skip_group_check=True),
                reads=(brr, B("consts")), writes=(bpsB,))
            tk.emit(tk.dve, lambda: nc.vector.tensor_copy(bvw, psB),
                    reads=(bpsB,), writes=(bbv,))
            tk.emit(tk.dve, lambda: nc.vector.tensor_scalar(
                out=b2w, in0=bvw, scalar1=-EPS, scalar2=1.0,
                op0=ALU.mult, op1=ALU.add), reads=(bbv,), writes=(bbv,))
            ypos = ysb[:, h * 512:(h + 1) * 512]
            tk.emit(tk.dve, lambda y=ypos: nc.vector.tensor_mul(y, psN, bvw),
                    reads=(bpsN, bbv), writes=(by,))
            tk.emit(tk.dve, lambda y=ypos, h_=h: nc.vector.scalar_tensor_tensor(
                out=y, in0=b2w, scalar=bias_sb[:, 16 + h_:17 + h_], in1=y,
                op0=ALU.mult, op1=ALU.add),
                reads=(bbv, bbias, by), writes=(by,))
        tk.emit(tk.dve, lambda: nc.vector.tensor_copy(yhi, ysb),
                reads=(by,), writes=(byhl,))
        tk.emit(tk.dve, lambda: nc.vector.tensor_sub(ylo, ysb, yhi),
                reads=(by,), writes=(byhl,))
        for dc in range(16):
            tk.emit(tk.sd, lambda d=dc: nc.sync.dma_start(
                out=wb, in_=t["l1wo_t"][d]), writes=(bwo,))
            tk.emit(tk.pool, lambda: nc.gpsimd.tensor_copy(wh, wb),
                    reads=(bwo,), writes=(bwh_,))
            tk.emit(tk.pool, lambda: nc.gpsimd.tensor_sub(wl, wb, wh),
                    reads=(bwo, bwh_), writes=(bwl_,))
            for c in range(16):
                lh = wh[:, c * 128:(c + 1) * 128]
                ll = wl[:, c * 128:(c + 1) * 128]
                rh = yhi[:, c * 512:(c + 1) * 512]
                rl = ylo[:, c * 512:(c + 1) * 512]
                tk.emit(tk.pe, lambda a=lh, b_=rh, s=(c == 0): nc.tensor.matmul(
                    psO, a, b_, start=s, stop=False, skip_group_check=True),
                    reads=(bwh_, byhl), writes=(bpsO,))
                tk.emit(tk.pe, lambda a=lh, b_=rl: nc.tensor.matmul(
                    psO, a, b_, start=False, stop=False, skip_group_check=True),
                    reads=(bwh_, byhl), writes=(bpsO,))
                tk.emit(tk.pe, lambda a=ll, b_=rh, s=(c == 15): nc.tensor.matmul(
                    psO, a, b_, start=False, stop=s, skip_group_check=True),
                    reads=(bwl_, byhl), writes=(bpsO,))
            tk.emit(tk.act, lambda d=dc: nc.scalar.activation(
                out=o32, in_=psO, func=AF.Identity, bias=bias_sb[:, d:d + 1]),
                reads=(bpsO, bbias), writes=(bo32,))
            for ts in range(4):
                alt = ts % 2
                tk.emit(tk.pe, lambda w=ts, a=alt: nc.tensor.transpose(
                    psT[a][:, 0:128], o32[:, w * 128:(w + 1) * 128], ctx.c_ident),
                    reads=(bo32, B("consts")), writes=(bpsT[alt],))
                dst = xres[:, ts * 2048 + dc * 128: ts * 2048 + dc * 128 + 128]
                tk.emit(tk.dve, lambda d=dst, a=alt: nc.vector.tensor_add(
                    d, d, psT[a][:, 0:128]), reads=(bpsT[alt], bxres), writes=(bxres,))
        tk.emit(tk.gd, lambda jj=j: nc.gpsimd.dma_start(
            out=t["x3_hbm"][jj * 512:(jj + 1) * 512, :].rearrange(
                "(s p) d -> p s d", p=128),
            in_=xres.rearrange("p (s d) -> p s d", s=4)),
            reads=(bxres,), writes=(bX3,))
        es.close()
        tk.barrier()
    es2.close()


def emit_phase3(ctx):
    nc, tk, t = ctx.nc, ctx.tk, ctx.t
    B = ctx.buf
    for j in range(4):
        es = ExitStack()

        def al(name, shape, dtype, _es=es, _j=j):
            return _es.enter_context(nc.sbuf_tensor(f"g{name}_{_j}", shape, dtype)).ap()

        def alp(name, _es=es, _j=j):
            return _es.enter_context(nc.psum_tensor(f"g{name}_{_j}", [128, 512], F32)).ap()

        xa = al("xa", [128, 4 * 2048], F32)
        xsq = al("xsq", [128, 2048], F32)
        stats = al("stats", [128, 32], F32)
        xnh = al("xnh", [128, 16 * 512], F32R)
        xnl = al("xnl", [128, 16 * 512], F32R)
        psT = [alp("psT0"), alp("psT1")]
        bxa, bxsq, bst = B("g_xa"), B("g_xsq"), B("g_st")
        bxnh, bxnl = B("g_xnh"), B("g_xnl")
        bpsT = [B("g_psT0"), B("g_psT1")]
        bXNH, bXNL = B("h_xnh"), B("h_xnl")
        tk.emit(tk.gd, lambda jj=j: nc.gpsimd.dma_start(
            out=xa.rearrange("p (s d) -> p s d", s=4),
            in_=t["x3_hbm"][jj * 512:(jj + 1) * 512, :].rearrange(
                "(s p) d -> p s d", p=128)), reads=(B("h_x3"),), writes=(bxa,))
        for sub in range(4):
            emit_ln(ctx, xa, bxa, xsq, bxsq, stats, bst, sub)
        emit_transpose_split(ctx, xa, bxa, psT, bpsT, (xnh, xnl), (bxnh, bxnl),
                             split=True)
        tk.emit(tk.gd, lambda jj=j: nc.gpsimd.dma_start(
            out=t["xnh_hbm"].rearrange("p (c d) -> p c d", c=16)[
                :, :, jj * 512:(jj + 1) * 512],
            in_=xnh.rearrange("p (c d) -> p c d", c=16)),
            reads=(bxnh,), writes=(bXNH,))
        tk.emit(tk.gd, lambda jj=j: nc.gpsimd.dma_start(
            out=t["xnl_hbm"].rearrange("p (c d) -> p c d", c=16)[
                :, :, jj * 512:(jj + 1) * 512],
            in_=xnl.rearrange("p (c d) -> p c d", c=16)),
            reads=(bxnl,), writes=(bXNL,))
        es.close()
        tk.barrier()

    for isuper in range(4):
        for j in range(4):
            es = ExitStack()

            def al(name, shape, dtype, _es=es, _i=isuper, _j=j):
                return _es.enter_context(
                    nc.sbuf_tensor(f"q{name}_{_i}_{_j}", shape, dtype)).ap()

            def alp(name, _es=es, _i=isuper, _j=j):
                return _es.enter_context(
                    nc.psum_tensor(f"q{name}_{_i}_{_j}", [128, 512], F32)).ap()

            xnh = al("xnh", [128, 16 * 512], F32R)
            xnl = al("xnl", [128, 16 * 512], F32R)
            wgi = al("wgi", [128, 2 * 2048], I32)
            wgr = al("wgr", [128, 2 * 2048], F32R)
            sgb = al("sgb", [128, 512], F32)
            h32 = al("h32", [128, 512], F32)
            hsh = al("hsh", [128, 16 * 512], F32R)
            hsl = al("hsl", [128, 16 * 512], F32R)
            scl = al("scl", [128, 64 * 2 + 16], F32)
            ei = al("ei", [128, 2 * 512], F32)
            esg = al("esg", [128, 2 * 512], F32)
            xres = al("xres", [128, 4 * 2048], F32) if isuper == 3 else None
            psG = [alp("psG0"), alp("psG1")]
            psU = [alp("psU0"), alp("psU1")]
            psT = [alp("psT0"), alp("psT1")] if isuper == 3 else [None, None]

            bxnh, bxnl = B("q_xnh"), B("q_xnl")
            bwg = [B("q_wg0"), B("q_wg1")]
            bwgr = [B("q_wgr0"), B("q_wgr1")]
            bsg = [B("q_sg0"), B("q_sg1")]
            bh32 = [B("q_h320"), B("q_h321")]
            bhs = B("q_hs")
            bscl = B("q_scl")
            bei = [B("q_ei0"), B("q_ei1")]
            bes = [B("q_es0"), B("q_es1")]
            bxres = B("q_xres")
            bpsG = [B("q_psG0"), B("q_psG1")]
            bpsU = [B("q_psU0"), B("q_psU1")]
            bpsT = [B("q_psT0"), B("q_psT1")]
            bE = B("h_e")
            bOUT = B("h_out")

            tk.emit(tk.sd, lambda: nc.sync.dma_start(
                out=scl[:, 0:64], in_=t["gsm"]), writes=(bscl,))
            tk.emit(tk.sd, lambda: nc.sync.dma_start(
                out=scl[:, 64:128], in_=t["usm"]), writes=(bscl,))
            tk.emit(tk.sd, lambda: nc.sync.dma_start(
                out=scl[:, 128:144], in_=t["dsm"]), writes=(bscl,))
            tk.emit(tk.sd, lambda jj=j: nc.sync.dma_start(
                out=xnh.rearrange("p (c d) -> p c d", c=16),
                in_=t["xnh_hbm"].rearrange("p (c d) -> p c d", c=16)[
                    :, :, jj * 512:(jj + 1) * 512]),
                reads=(B("h_xnh"),), writes=(bxnh,))
            tk.emit(tk.sd, lambda jj=j: nc.sync.dma_start(
                out=xnl.rearrange("p (c d) -> p c d", c=16),
                in_=t["xnl_hbm"].rearrange("p (c d) -> p c d", c=16)[
                    :, :, jj * 512:(jj + 1) * 512]),
                reads=(B("h_xnl"),), writes=(bxnl,))
            if isuper == 3:
                tk.emit(tk.gd, lambda jj=j: nc.gpsimd.dma_start(
                    out=xres.rearrange("p (s d) -> p s d", s=4),
                    in_=t["x3_hbm"][jj * 512:(jj + 1) * 512, :].rearrange(
                        "(s p) d -> p s d", p=128)),
                    reads=(B("h_x3"),), writes=(bxres,))
            for ic in range(16):
                i = isuper * 16 + ic
                sl = ic % 2
                pp = ic % 2
                wsl = wgi[:, sl * 2048: sl * 2048 + 2048]
                wrs = wgr[:, sl * 2048: sl * 2048 + 2048]
                tk.emit(tk.sd, lambda n=i, w=wsl: nc.sync.dma_start(
                    out=w, in_=t["wg_t"][n]), writes=(bwg[sl],))
                tk.emit(tk.act, lambda a=wrs, w=wsl: nc.scalar.activation(
                    out=a, in_=w, func=AF.Identity, bias=ctx.c_m128[:, 0:1]),
                    reads=(bwg[sl], B("consts")), writes=(bwgr[sl],))
                for c in range(16):
                    lw = wgr[:, sl * 2048 + c * 128: sl * 2048 + c * 128 + 128]
                    rh = xnh[:, c * 512:(c + 1) * 512]
                    rl = xnl[:, c * 512:(c + 1) * 512]
                    tk.emit(tk.pe, lambda a=lw, b_=rh, s=(c == 0), p=pp: nc.tensor.matmul(
                        psG[p], a, b_, start=s, stop=False, skip_group_check=True),
                        reads=(bwgr[sl], bxnh), writes=(bpsG[pp],))
                    tk.emit(tk.pe, lambda a=lw, b_=rl, s=(c == 15), p=pp: nc.tensor.matmul(
                        psG[p], a, b_, start=False, stop=s, skip_group_check=True),
                        reads=(bwgr[sl], bxnl), writes=(bpsG[pp],))
                tk.emit(tk.sd, lambda n=i, w=wsl: nc.sync.dma_start(
                    out=w, in_=t["wu_t"][n]), reads=(bwgr[sl],), writes=(bwg[sl],))
                tk.emit(tk.act, lambda a=wrs, w=wsl: nc.scalar.activation(
                    out=a, in_=w, func=AF.Identity, bias=ctx.c_m128[:, 0:1]),
                    reads=(bwg[sl], B("consts")), writes=(bwgr[sl],))
                for c in range(16):
                    lw = wgr[:, sl * 2048 + c * 128: sl * 2048 + c * 128 + 128]
                    rh = xnh[:, c * 512:(c + 1) * 512]
                    rl = xnl[:, c * 512:(c + 1) * 512]
                    tk.emit(tk.pe, lambda a=lw, b_=rh, s=(c == 0), p=pp: nc.tensor.matmul(
                        psU[p], a, b_, start=s, stop=False, skip_group_check=True),
                        reads=(bwgr[sl], bxnh), writes=(bpsU[pp],))
                    tk.emit(tk.pe, lambda a=lw, b_=rl, s=(c == 15), p=pp: nc.tensor.matmul(
                        psU[p], a, b_, start=False, stop=s, skip_group_check=True),
                        reads=(bwgr[sl], bxnl), writes=(bpsU[pp],))
                sgs = sgb
                h3 = h32
                tk.emit(tk.act, lambda d=sgs, n=i, p=pp: nc.scalar.activation(
                    out=d, in_=psG[p], func=AF.Silu, scale=scl[:, n:n + 1]),
                    reads=(bpsG[pp], bscl), writes=(bsg[0],))
                tk.emit(tk.dve, lambda d=h3, s=sgs, n=i, p=pp:
                        nc.vector.scalar_tensor_tensor(
                            out=d, in0=psU[p], scalar=scl[:, 64 + n:64 + n + 1],
                            in1=s, op0=ALU.mult, op1=ALU.mult),
                        reads=(bpsU[pp], bsg[0], bscl), writes=(bh32[0],))
                tk.emit(tk.dve, lambda d=hsh, c=ic, s=h3: nc.vector.tensor_copy(
                    d[:, c * 512:(c + 1) * 512], s),
                    reads=(bh32[0],), writes=(bhs,))
                tk.emit(tk.pool, lambda d=hsl, c=ic, s=h3, hh=hsh: nc.gpsimd.tensor_sub(
                    d[:, c * 512:(c + 1) * 512], s, hh[:, c * 512:(c + 1) * 512]),
                    reads=(bh32[0], bhs), writes=(bhs,))
            for hc in range(16):
                sl = hc % 2
                pp = hc % 2
                wsl = wgi[:, sl * 2048: sl * 2048 + 2048]
                wrs = wgr[:, sl * 2048: sl * 2048 + 2048]
                tk.emit(tk.sd, lambda h=hc, n=isuper, w=wsl: nc.sync.dma_start(
                    out=w, in_=t["wd_t"][h][:, n * 2048:(n + 1) * 2048]),
                    reads=(bwgr[sl],), writes=(bwg[sl],))
                tk.emit(tk.act, lambda a=wrs, w=wsl: nc.scalar.activation(
                    out=a, in_=w, func=AF.Identity, bias=ctx.c_m128[:, 0:1]),
                    reads=(bwg[sl], B("consts")), writes=(bwgr[sl],))
                for ic in range(16):
                    lw = wgr[:, sl * 2048 + ic * 128: sl * 2048 + ic * 128 + 128]
                    rh = hsh[:, ic * 512:(ic + 1) * 512]
                    rl = hsl[:, ic * 512:(ic + 1) * 512]
                    tk.emit(tk.pe, lambda a=lw, b_=rh, s=(ic == 0), p=pp: nc.tensor.matmul(
                        psG[p], a, b_, start=s, stop=False, skip_group_check=True),
                        reads=(bwgr[sl], bhs), writes=(bpsG[pp],))
                    tk.emit(tk.pe, lambda a=lw, b_=rl, s=(ic == 15), p=pp: nc.tensor.matmul(
                        psG[p], a, b_, start=False, stop=s, skip_group_check=True),
                        reads=(bwgr[sl], bhs), writes=(bpsG[pp],))
                esl = esg[:, sl * 512: sl * 512 + 512]
                eis = ei[:, sl * 512: sl * 512 + 512]
                ecol = lambda h, jj: slice(h * 2048 + jj * 512, h * 2048 + jj * 512 + 512)
                if isuper == 0:
                    tk.emit(tk.dve, lambda d=esl, p=pp: nc.vector.tensor_copy(
                        d, psG[p]), reads=(bpsG[pp],), writes=(bes[sl],))
                    tk.emit(tk.gd, lambda s=esl, h=hc, jj=j: nc.gpsimd.dma_start(
                        out=t["e_hbm"][:, ecol(h, jj)], in_=s),
                        reads=(bes[sl],), writes=(bE,))
                elif isuper < 3:
                    tk.emit(tk.sd, lambda d=eis, h=hc, jj=j: nc.sync.dma_start(
                        out=d, in_=t["e_hbm"][:, ecol(h, jj)]),
                        reads=(bE,), writes=(bei[sl],))
                    tk.emit(tk.dve, lambda d=esl, i_=eis, p=pp: nc.vector.tensor_add(
                        d, i_, psG[p]), reads=(bei[sl], bpsG[pp]), writes=(bes[sl],))
                    tk.emit(tk.gd, lambda s=esl, h=hc, jj=j: nc.gpsimd.dma_start(
                        out=t["e_hbm"][:, ecol(h, jj)], in_=s),
                        reads=(bes[sl],), writes=(bE,))
                else:
                    tk.emit(tk.sd, lambda d=eis, h=hc, jj=j: nc.sync.dma_start(
                        out=d, in_=t["e_hbm"][:, ecol(h, jj)]),
                        reads=(bE,), writes=(bei[sl],))
                    tk.emit(tk.dve, lambda d=esl, i_=eis, p=pp: nc.vector.tensor_add(
                        d, i_, psG[p]), reads=(bei[sl], bpsG[pp]), writes=(bes[sl],))
                    tk.emit(tk.dve, lambda d=esl, h=hc: nc.vector.tensor_scalar_mul(
                        d, d, scl[:, 128 + h:129 + h]),
                        reads=(bscl, bes[sl]), writes=(bes[sl],))
                    for ts in range(4):
                        alt = ts % 2
                        tk.emit(tk.pe, lambda s=esl, w=ts, a=alt: nc.tensor.transpose(
                            psT[a][:, 0:128], s[:, w * 128:(w + 1) * 128],
                            ctx.c_ident),
                            reads=(bes[sl], B("consts")), writes=(bpsT[alt],))
                        dst = xres[:, ts * 2048 + hc * 128: ts * 2048 + hc * 128 + 128]
                        tk.emit(tk.dve, lambda d=dst, a=alt: nc.vector.tensor_add(
                            d, d, psT[a][:, 0:128]),
                            reads=(bpsT[alt], bxres), writes=(bxres,))
            if isuper == 3:
                tk.emit(tk.gd, lambda jj=j: nc.gpsimd.dma_start(
                    out=t["yout"][jj * 512:(jj + 1) * 512, :].rearrange(
                        "(s p) d -> p s d", p=128),
                    in_=xres.rearrange("p (s d) -> p s d", s=4)),
                    reads=(bxres,), writes=(bOUT,))
            es.close()
            tk.barrier()


def finish(ctx, out_bufs):
    nc, tk = ctx.nc, ctx.tk
    tk.emit(tk.pool, lambda: nc.gpsimd.nop(), reads=tuple(out_bufs), writes=())
    tk.flush()


def declare_common(ctx, io):
    def k(name, default):
        return io.get(name, default)
    EI = "ExternalInput"
    c = ctx
    c.dram("xseq", [SEQ, D], F32, kind=EI)
    c.dram("xown", [2048, D], F32, kind=EI)
    c.dram("masks", [4, 8, 128, 512], F32, kind=EI)
    c.dram("wq_t", [16, 128, 2048], F32, kind=EI)
    c.dram("wk_t", [4, 128, 2048], F32, kind=EI)
    c.dram("wv_nat", [16, 128, 512], F32, kind=EI)
    c.dram("wo_t", [16, 128, 2048], F32, kind=EI)
    c.dram("w0_t", [64, 128, 2048], F32, kind=EI)
    c.dram("b0m", [128, 64], F32, kind=EI)
    c.dram("w1_t", [16, 128, 8192], F32, kind=EI)
    c.dram("b1m", [128, 16], F32, kind=EI)
    c.dram("l1wq_t", [16, 128, 2048], F32, kind=EI)
    c.dram("l1wk_t", [16, 128, 2048], F32, kind=EI)
    c.dram("l1wv_t", [16, 128, 2048], F32, kind=EI)
    c.dram("l1wo_t", [16, 128, 2048], F32, kind=EI)
    c.dram("l1bqm", [128, 16], F32, kind=EI)
    c.dram("l1bkm", [128, 16], F32, kind=EI)
    c.dram("l1bom", [128, 16], F32, kind=EI)
    c.dram("l1bvm", [128, 16], F32, kind=EI)
    c.dram("mapq_T", [128, 64], F32, kind=EI)
    c.dram("mapk_T", [128, 64], F32, kind=EI)
    c.dram("wg_t", [64, 128, 2048], I32, kind=EI)
    c.dram("wu_t", [64, 128, 2048], I32, kind=EI)
    c.dram("wd_t", [16, 128, 8192], I32, kind=EI)
    c.dram("gsm", [128, 64], F32, kind=EI)
    c.dram("usm", [128, 64], F32, kind=EI)
    c.dram("dsm", [128, 16], F32, kind=EI)
    c.dram("q_hbm", [128, 16 * 2048], F32, kind=k("q_hbm", "Internal"))
    c.dram("kh_hbm", [64, 8 * 4096], F32R, kind=k("kh_hbm", "Internal"))
    c.dram("kl_hbm", [64, 8 * 4096], F32R, kind=k("kl_hbm", "Internal"))
    c.dram("vh_hbm", [4096, 512], F32R, kind=k("vh_hbm", "Internal"))
    c.dram("vl_hbm", [4096, 512], F32R, kind=k("vl_hbm", "Internal"))
    c.dram("x1_hbm", [2048, D], F32, kind=k("x1_hbm", "Internal"))
    c.dram("x2_hbm", [2048, D], F32, kind=k("x2_hbm", "Internal"))
    c.dram("x3_hbm", [2048, D], F32, kind=k("x3_hbm", "Internal"))
    c.dram("e_hbm", [128, 16 * 2048], F32, kind=k("e_hbm", "Internal"))
    c.dram("phiq_hbm", [64, 16 * 2048], F32, kind=k("phiq_hbm", "Internal"))
    c.dram("xnh_hbm", [128, 16 * 2048], F32R, kind=k("xnh_hbm", "Internal"))
    c.dram("xnl_hbm", [128, 16 * 2048], F32R, kind=k("xnl_hbm", "Internal"))
    c.dram("kv_in", [64, 16 * 132], F32)
    c.dram("kv_out", [64, 16 * 132], F32)
    c.dram("yout", [2048, D], F32, kind="ExternalOutput")


PHASES = {
    "p0": lambda ctx: (emit_phase0a(ctx), emit_phase0b(ctx)),
    "p1": emit_phase1,
    "p2": emit_phase2,
    "p3": emit_phase3,
}
FINAL_BUF = {"p0": "h_x1", "p1": "h_x2", "p2": "h_x3", "p3": "h_out"}


def build(phases, io=None):
    ctx = Ctx()
    declare_common(ctx, io or {})
    ctx.load_consts()
    for ph in phases:
        PHASES[ph](ctx)
    finish(ctx, [ctx.buf(FINAL_BUF[phases[-1]])])
    return ctx


def host_inputs(inputs, core):
    b, parity = core // 2, core % 2
    x = np.ascontiguousarray(np.asarray(inputs["L0_attn_x"][b], dtype=np.float32))
    xown = np.ascontiguousarray(
        np.concatenate([x[q0:q0 + 512] for q0 in QOFF[parity]], axis=0))
    f32 = lambda a: np.asarray(a, np.float32)
    i32 = lambda a: np.ascontiguousarray(np.asarray(a, np.int32))
    return {
        "xseq": x,
        "xown": xown,
        "masks": make_masks(parity),
        "wq_t": host_tile(f32(inputs["L0_attn_wq"])),
        "wk_t": host_tile(f32(inputs["L0_attn_wk"])),
        "wv_nat": np.ascontiguousarray(f32(inputs["L0_attn_wv"]).reshape(16, 128, 512)),
        "wo_t": host_tile(f32(inputs["L0_attn_wo"])),
        "w0_t": host_tile(np.ascontiguousarray(f32(inputs["L0_ffn_W0"]).T)),
        "b0m": host_cols(f32(inputs["L0_ffn_bias0"])),
        "w1_t": host_tile(np.ascontiguousarray(f32(inputs["L0_ffn_W1"]).T)),
        "b1m": host_cols(f32(inputs["L0_ffn_bias1"])),
        "l1wq_t": host_tile(np.ascontiguousarray(f32(inputs["L1_attn_wq"]).T)),
        "l1wk_t": host_tile(np.ascontiguousarray(f32(inputs["L1_attn_wk"]).T)),
        "l1wv_t": host_tile(np.ascontiguousarray(f32(inputs["L1_attn_wv"]).T)),
        "l1wo_t": host_tile(np.ascontiguousarray(f32(inputs["L1_attn_wo"]).T)),
        "l1bqm": host_cols(f32(inputs["L1_attn_bq"])),
        "l1bkm": host_cols(f32(inputs["L1_attn_bk"])),
        "l1bom": host_cols(f32(inputs["L1_attn_bo"])),
        "l1bvm": host_cols(f32(inputs["L1_attn_bv"])),
        "mapq_T": np.ascontiguousarray(f32(inputs["L1_attn_wmap_q"]).T),
        "mapk_T": np.ascontiguousarray(f32(inputs["L1_attn_wmap_k"]).T),
        "wg_t": host_tile(i32(np.asarray(inputs["L1_ffn_gate_w"]).T)),
        "wu_t": host_tile(i32(np.asarray(inputs["L1_ffn_up_w"]).T)),
        "wd_t": host_tile(i32(np.asarray(inputs["L1_ffn_down_w"]).T)),
        "gsm": host_cols(f32(inputs["L1_ffn_gate_scale"])),
        "usm": host_cols(f32(inputs["L1_ffn_up_scale"])),
        "dsm": host_cols(f32(inputs["L1_ffn_down_scale"])),
    }


def gather_output(inputs, results):
    out = np.empty((4, SEQ, D), dtype=np.float32)
    for c in range(8):
        b, parity = c // 2, c % 2
        y = results[c]["yout"]
        for j, q0 in enumerate(QOFF[parity]):
            out[b, q0:q0 + 512] = y[j * 512:(j + 1) * 512]
    return out


_LAUNCH_GROUPS = [["p0", "p1", "p2", "p3"]]


def kernel(**inputs):
    in_maps = [host_inputs(inputs, c) for c in range(8)]
    results = [dict() for _ in range(8)]
    carry = {}
    for gi, group in enumerate(_LAUNCH_GROUPS):
        first, last = group[0], group[-1]
        io = {}
        carry_in = {"p1": ["x1_hbm"], "p2": ["x2_hbm"], "p3": ["x3_hbm"]}.get(first, [])
        carry_out = {"p0": ["x1_hbm"], "p1": ["x2_hbm"], "p2": ["x3_hbm"]}.get(last, [])
        for n in carry_in:
            io[n] = "ExternalInput"
        for n in carry_out:
            io[n] = "ExternalOutput"
        ctx = build(group, io)
        ims = []
        for c in range(8):
            im = dict(in_maps[c])
            for n in carry_in:
                im[n] = carry[n][c]
            ims.append(im)
        res = run_bass_kernel_spmd(ctx.nc, ims, core_ids=list(range(8)))
        for c in range(8):
            for n in carry_out:
                carry.setdefault(n, {})[c] = np.asarray(res.results[c][n])
            if "yout" in res.results[c]:
                results[c]["yout"] = np.asarray(res.results[c]["yout"])
    return gather_output(inputs, results)
